# version 28
# baseline (speedup 1.0000x reference)
import numpy as np
from contextlib import ExitStack

import concourse.bass as bass
import concourse.mybir as mybir
from concourse.bass_utils import run_bass_kernel_spmd

F32 = mybir.dt.float32
BF16 = mybir.dt.bfloat16
AF = mybir.ActivationFunctionType
ALU = mybir.AluOpType

SAFE = False

P = 128
L = 2048
DM = 1024
NDC = DM // P
FF = 4096
NFC = FF // P
TTOK = 256
NT = L // TTOK
EPS = 1e-5

B_SPIKE = [10000.0 / 40951.0, 19000.0 / 40951.0, 28100.0 / 40951.0,
           37100.0 / 40951.0, 44390.0 / 40951.0]


def _sched_entry(engine, build, ev=None, waits=(), dma_sem=None):
    return {"engine": engine, "build": build, "ev": ev, "waits": tuple(waits),
            "dma_sem": dma_sem}


def build_nc():
    nc = bass.Bass("TRN2", target_bir_lowering=False,
                   dynamic_dma_scratch_size=8192,
                   detect_race_conditions=SAFE)

    xT = nc.dram_tensor("xT", [DM, L], F32, kind="ExternalInput").ap()
    bwT = nc.dram_tensor("bwT", [DM, P], F32, kind="ExternalInput").ap()
    cwT = nc.dram_tensor("cwT", [P, DM], F32, kind="ExternalInput").ap()
    logA = nc.dram_tensor("logA", [P, 1], F32, kind="ExternalInput").ap()
    dre = nc.dram_tensor("dre", [P, NDC], F32, kind="ExternalInput").ap()
    wfcT = nc.dram_tensor("wfcT", [DM, FF], F32, kind="ExternalInput").ap()
    wprojT = nc.dram_tensor("wprojT", [FF, DM], F32, kind="ExternalInput").ap()
    out_d = nc.dram_tensor("out", [DM, L], F32, kind="ExternalOutput").ap()

    with ExitStack() as ctx:
        def sb(name, shape, dt):
            return ctx.enter_context(nc.sbuf_tensor(name, shape, dt))

        def psum(name):
            return ctx.enter_context(nc.psum_tensor(name, [P, 512], F32))

        def sem(name):
            return ctx.enter_context(nc.semaphore(name))

        wfc16 = sb("wfc16", [P, NDC * FF], BF16)
        wpj16 = sb("wpj16", [P, NFC * DM], BF16)
        bw16 = sb("bw16", [P, DM], BF16)
        cw16 = sb("cw16", [P, DM], BF16)
        logA_sb = sb("logA_sb", [P, 1], F32)
        acol = sb("acol", [P, 1], F32)
        dre_sb = sb("dre_sb", [P, NDC], F32)
        d1p = sb("d1p", [P, NDC], F32)
        ones_c = sb("ones_c", [P, 1], BF16)
        eps_c = sb("eps_c", [P, 1], F32)
        ones_r = sb("ones_r", [1, P], BF16)
        arena16 = sb("arena16", [P, 16384], BF16)
        xb16 = arena16[:, 0:16384]
        h1 = arena16[:, 0:8192]
        r2c16f = arena16[:, 8192:10240]
        r2sq16f = arena16[:, 10240:12288]
        r16d = [arena16[:, 12288 + i * 256:12288 + (i + 1) * 256]
                for i in range(2)] + \
               [arena16[:, 11264 + i * 256:11264 + (i + 1) * 256]
                for i in range(2)]
        rsq16d = [arena16[:, 12800 + i * 256:12800 + (i + 1) * 256]
                  for i in range(2)] + \
                 [arena16[:, 11776 + i * 256:11776 + (i + 1) * 256]
                  for i in range(2)]
        rr = [arena16[:, 13312 + i * 256:13312 + (i + 1) * 256]
              for i in range(4)] + [
              arena16[:, 15360 + i * 256:15360 + (i + 1) * 256]
              for i in range(2)]
        mi16 = [arena16[0:1, 14336 + i * 512:14336 + (i + 1) * 512]
                for i in range(2)]
        arena32 = sb("arena32", [P, 2816], F32)
        xp = arena32[:, 0:2048]
        vsb = [arena32[:, 0 + i * 256:0 + (i + 1) * 256] for i in range(2)]
        sacc = [arena32[:, 512 + i * 256:512 + (i + 1) * 256] for i in range(2)]
        xs_d = [arena32[:, 1024 + i * 256:1024 + (i + 1) * 256] for i in range(2)]
        stA = [arena32[0:1, 1536 + i * 256:1536 + (i + 1) * 256] for i in range(2)]
        stB = [arena32[0:1, 2048 + i * 256:2048 + (i + 1) * 256] for i in range(2)]
        tmp_row = arena32[0:1, 2560:2816]
        out_st = vsb
        hs16 = sb("hs16", [P, L], BF16)
        r_sb = sb("r_sb", [P, NDC * TTOK], F32)
        r2_sb = sb("r2_sb", [P, NDC * TTOK], F32)
        x1 = [sb("x1_0", [P, NDC * TTOK], F32),
              sb("x1_1", [P, NDC * TTOK], F32)]
        x1b = sb("x1b", [P, NDC * TTOK], BF16)

        pbf = [psum("pbf0"), psum("pbf1")]
        pby = [psum("pby0"), psum("pby1")]
        pb_s1 = psum("pb_s1")
        pb_s2 = psum("pb_s2")
        pb_b1 = psum("pb_b1")
        pb_b2 = psum("pb_b2")

        s_pe = sem("s_pe")
        s_gp = sem("s_gp")
        s_ve = sem("s_ve")
        s_ac = sem("s_ac")
        s_dw1 = sem("s_dw1")
        s_dwf = sem("s_dwf")
        s_dwp = sem("s_dwp")
        s_c = sem("s_c")
        s_x0 = sem("s_x0")
        s_x1 = sem("s_x1")
        s_o = [sem("s_o0"), sem("s_o1")]

        SEMS = {"pe": s_pe, "gp": s_gp, "ve": s_ve, "ac": s_ac, "dw1": s_dw1,
                "dwf": s_dwf, "dwp": s_dwp, "c": s_c, "x0": s_x0,
                "x1": s_x1, "o0": s_o[0], "o1": s_o[1]}

        sched = []

        ent_ctx = {"label": ""}

        def ent(engine, build, ev=None, waits=(), dma_sem=None):
            e = _sched_entry(engine, build, ev, waits, dma_sem)
            e["label"] = ev or ent_ctx["label"]
            sched.append(e)

        ent("gp", lambda g: g.dma_start(
            out=bw16[:].rearrange("p (c n) -> p c n", c=NDC),
            in_=bwT.rearrange("(c p) n -> p c n", p=P)),
            ev="dw_bw", dma_sem="dw1")
        ent("gp", lambda g: g.dma_start(out=cw16[:], in_=cwT),
            ev="dw_cw", dma_sem="dw1")
        ent("gp", lambda g: g.dma_start(
            out=xb16.rearrange("p (c t) -> p c t", c=NDC),
            in_=xT.rearrange("(c p) t -> p c t", p=P)),
            ev="dw_xb", dma_sem="dw1")
        for q in range(NDC):
            src = wfcT.rearrange("(c p) f -> p c f", p=P)[:, q:q + 1, :]
            dst = wfc16[:, q * FF:(q + 1) * FF]
            ent("gp", (lambda g, s=src, d=dst: g.dma_start(
                out=d.rearrange("p (c f) -> p c f", c=1), in_=s)),
                ev=(f"dw_wfc" if q == NDC - 1 else f"dw_wfc{q}"), dma_sem="dwf")
        for q in range(8):
            qs = NFC // 8
            src = wprojT.rearrange("(c p) d -> p c d", p=P)[:, q * qs:(q + 1) * qs, :]
            dst = wpj16[:, q * qs * DM:(q + 1) * qs * DM]
            ent("gp", (lambda g, s=src, d=dst: g.dma_start(out=d, in_=s)),
                ev=f"dw_wp{q}", dma_sem="dwp")
        ent("sy", lambda s: s.dma_start(out=logA_sb[:], in_=logA),
            ev="c_logA", dma_sem="c")
        ent("sy", lambda s: s.dma_start(out=dre_sb[:], in_=dre),
            ev="c_dre", dma_sem="c")
        ent("sy", lambda s: s.dma_start(
            out=r_sb[:].rearrange("p (c t) -> p c t", t=TTOK),
            in_=xT.rearrange("(c p) t -> p c t", p=P)[:, :, 0:TTOK]),
            ev="c_xr0", dma_sem="c")


        ent("ve", lambda v: v.memset(ones_c[:], 1.0), ev="ones_c")
        ent("ve", lambda v: v.memset(eps_c[:], EPS), ev="eps_c")
        ent("ve", lambda v: v.memset(ones_r[:], 1.0), ev="ones_r")
        ent("ac", lambda a: a.activation(out=acol[:], in_=logA_sb[:], func=AF.Exp),
            ev="A", waits=["c_xr0"])
        ent("ac", lambda a: a.activation(out=d1p[:], in_=dre_sb[:],
                                         func=AF.Identity, bias=1.0),
            ev="D1", waits=["c_xr0"])

        for tq in range(4):
            waits = ["dw_xb"] if tq == 0 else [f"xpev{tq - 2}"] if tq >= 2 else []
            for dc in range(NDC):
                w = list(waits) if dc == 0 else []
                ent("pe", (lambda t, tq=tq, dc=dc: t.matmul(
                    pbf[tq % 2][:, 0:512],
                    bw16[:, dc * P:(dc + 1) * P],
                    xb16[:, dc * L + tq * 512: dc * L + (tq + 1) * 512],
                    start=(dc == 0), stop=(dc == NDC - 1))),
                    ev=(f"xpmm{tq}" if dc == NDC - 1 else None), waits=w)
            ent("ac", (lambda a, tq=tq: a.activation(
                out=xp[:, tq * 512:(tq + 1) * 512], in_=pbf[tq % 2][:, 0:512],
                func=AF.Copy)), ev=f"xpev{tq}", waits=[f"xpmm{tq}"])


        ent("ve", lambda v: v.tensor_tensor_scan(
            out=hs16[:], data0=acol[:].broadcast_to([P, L]), data1=xp[:],
            initial=0.0, op0=ALU.mult, op1=ALU.add),
            ev="scan", waits=["xpev3", "A"])


        def pe_y(tau, dcs=range(NDC)):
            for dc in dcs:
                slot = dc if dc < 6 else dc - 6
                w = []
                if tau == 0 and dc == 0:
                    w.append("scan")
                if dc >= 6:
                    w.append(f"epi({tau},{dc - 6})")
                elif tau > 0:
                    w.append(f"epi({tau - 1},{dc + 6})" if dc < 2 else None)
                if slot < 4:
                    half = (slot // 2) % 2
                    if tau > 0 and half == 0:
                        w.append(f"mq1({tau - 1})")
                    if tau > 1 and half == 1:
                        w.append(f"mq2({tau - 2})")
                    dst = YB[slot % 2][:, half * 256:half * 256 + 256]
                else:
                    if tau > 0:
                        w.append(f"ln1({tau - 1},{NDC - 1})")
                    dst = pb_b1[:, (slot - 4) * 256:(slot - 4) * 256 + 256]
                w = [x for x in w if x]
                ent("pe", (lambda t, tau=tau, dc=dc, dst=dst: t.matmul(
                    dst,
                    cw16[:, dc * P:(dc + 1) * P],
                    hs16[:, tau * TTOK:(tau + 1) * TTOK],
                    start=True, stop=True)),
                    ev=f"y({tau},{dc})", waits=w)

        def dve_epi(tau, dcs=range(NDC)):
            for dc in dcs:
                slot = dc if dc < 6 else dc - 6
                if slot < 4:
                    half = (slot // 2) % 2
                    src = YB[slot % 2][:, half * 256:half * 256 + 256]
                else:
                    src = pb_b1[:, (slot - 4) * 256:(slot - 4) * 256 + 256]
                sl = slice(dc * TTOK, (dc + 1) * TTOK)
                if tau == 0:
                    w = [f"y({tau},{dc})", "D1", "c_xr0"]
                    ent("ve", (lambda v, dc=dc, sl=sl: v.tensor_scalar(
                        out=r_sb[:, sl], in0=r_sb[:, sl],
                        scalar1=d1p[:, dc:dc + 1], scalar2=None, op0=ALU.mult)),
                        waits=w)
                else:
                    w = [f"y({tau},{dc})", "D1", f"xsd({tau},{dc})"]
                    ent("ve", (lambda v, dc=dc, sl=sl: v.tensor_scalar(
                        out=r_sb[:, sl], in0=xs_d[dc % 2][:],
                        scalar1=d1p[:, dc:dc + 1], scalar2=None, op0=ALU.mult)),
                        waits=w)
                ent("ve", (lambda v, sl=sl, src=src: v.tensor_tensor(
                    out=r_sb[:, sl], in0=r_sb[:, sl],
                    in1=src, op=ALU.add)),
                    ev=f"epi({tau},{dc})")

        def act_rcasts(tau, dcs=range(NDC)):
            for dc in dcs:
                sl = slice(dc * TTOK, (dc + 1) * TTOK)
                w = [f"epi({tau},{dc})"]
                if tau > 0 or dc >= 4:
                    w.append(f"st1mm({tau},{dc - 4})" if dc >= 4
                             else f"st1mm({tau - 1},{dc + 4})")
                ent("ac", (lambda a, dc=dc, sl=sl: a.activation(
                    out=r16d[dc % 4][:], in_=r_sb[:, sl], func=AF.Copy)),
                    waits=w)
                ent("ac", (lambda a, dc=dc, sl=sl: a.activation(
                    out=rsq16d[dc % 4][:], in_=r_sb[:, sl], func=AF.Square)),
                    ev=f"r16({tau},{dc})")

        def pe_stats1(tau, dcs=range(NDC)):
            for dc in dcs:
                w = [f"r16({tau},{dc})"]
                if dc == 0:
                    w.append(f"epi({tau},{NDC - 1})")
                    w.append(f"mq1({tau - 1})" if tau > 0 else None)
                w = [x for x in w if x]
                ent("pe", (lambda t, dc=dc: t.matmul(
                    pb_s1[0:1, 0:256], ones_c[:], r16d[dc % 4][:],
                    start=(dc == 0), stop=(dc == NDC - 1))), waits=w)
                ent("pe", (lambda t, dc=dc: t.matmul(
                    pb_s2[0:1, 0:256], ones_c[:], rsq16d[dc % 4][:],
                    start=(dc == 0), stop=(dc == NDC - 1))),
                    ev=f"st1mm({tau},{dc})")

        def math_ln_a(tau, which):
            off = 0 if which == 1 else 256
            i = which - 1
            suf = f"{which}({tau})"
            ent("ac", (lambda a: a.activation(
                out=stA[i][:], in_=pb_s1[0:1, off:off + 256], func=AF.Copy,
                scale=1.0 / DM)), waits=[f"st{which}mm({tau},{NDC - 1})"])
            ent("ac", (lambda a: a.activation(
                out=mi16[i][0:1, 0:TTOK], in_=stA[i][:], func=AF.Copy)))
            ent("ac", (lambda a: a.activation(
                out=stB[i][:], in_=pb_s2[0:1, off:off + 256], func=AF.Copy,
                scale=1.0 / DM)), ev=f"mq{suf}")

        def math_ln_b(tau, which):
            i = which - 1
            suf = f"{which}({tau})"
            ent("ve", (lambda v, i=i: v.tensor_tensor(
                out=tmp_row[:], in0=stA[i][:], in1=stA[i][:], op=ALU.mult)),
                waits=[f"mq{suf}"])
            ent("ve", (lambda v, i=i: v.tensor_tensor(
                out=stB[i][:], in0=stB[i][:], in1=tmp_row[:], op=ALU.subtract)),
                ev=f"var{suf}")

        def math_ln_c(tau, which):
            i = which - 1
            suf = f"{which}({tau})"
            ent("ac", (lambda a, i=i: a.activation(
                out=tmp_row[:], in_=stB[i][:], func=AF.Ln, bias=eps_c[0:1, :])),
                waits=[f"var{suf}"])
            ent("ac", (lambda a, i=i: a.activation(
                out=stA[i][:], in_=tmp_row[:], func=AF.Exp, scale=-0.5)))
            ent("ac", (lambda a, i=i: a.activation(
                out=mi16[i][0:1, TTOK:2 * TTOK], in_=stA[i][:], func=AF.Copy)),
                ev=f"lm{suf}")

        def math_ln(tau, which):
            math_ln_a(tau, which)
            math_ln_b(tau, which)
            math_ln_c(tau, which)

        def pe_bcast(tau, which):
            pb = pb_b1 if which == 1 else pb_b2
            i = which - 1
            w = [f"lm{which}({tau})"]
            if which == 1 and tau > 0:
                w.append(f"ln1({tau - 1},{NDC - 1})")
            if which == 1:
                w.append(f"epi({tau},5)")
            if which == 2 and tau > 0:
                w.append(f"ln2({tau - 1},{NDC - 1})")
            ent("pe", (lambda t: t.matmul(
                pb[:, 0:256], ones_r[:], mi16[i][0:1, 0:TTOK],
                start=True, stop=True)), waits=w)
            ent("pe", (lambda t: t.matmul(
                pb[:, 256:512], ones_r[:], mi16[i][0:1, TTOK:2 * TTOK],
                start=True, stop=True)), ev=f"bc{which}({tau})")

        def dve_ln1_apply(tau, dcs=range(NDC)):
            xw = x1[tau % 2]
            for dc in dcs:
                sl = slice(dc * TTOK, (dc + 1) * TTOK)
                w = [f"bc1({tau})"] if dc == dcs[0] else []
                ent("ve", (lambda v, sl=sl, xw=xw: v.tensor_tensor(
                    out=xw[:, sl], in0=r_sb[:, sl], in1=pb_b1[:, 0:256],
                    op=ALU.subtract)), waits=w)
                ent("ve", (lambda v, sl=sl, xw=xw: v.tensor_tensor(
                    out=xw[:, sl], in0=xw[:, sl], in1=pb_b1[:, 256:512],
                    op=ALU.mult)), ev=f"ln1({tau},{dc})")

        def act_x1b(tau, dcs=range(NDC)):
            xw = x1[tau % 2]
            for dc in dcs:
                sl = slice(dc * TTOK, (dc + 1) * TTOK)
                ent("ac", (lambda a, sl=sl, xw=xw: a.activation(
                    out=x1b[:, sl], in_=xw[:, sl], func=AF.Copy)),
                    ev=f"x1b({tau},{dc})", waits=[f"ln1({tau},{dc})"])

        def pe_ffn1_fc(tau, fc):
            w = []
            if fc == 0:
                w += [f"x1b({tau},{NDC - 1})", "dw_wfc"]
            if fc < 4 and tau > 0:
                w.append(f"vev({tau - 1},{fc + 4})")
            elif fc >= 4:
                w.append(f"relu({tau},{fc - 4})")
            for dc in range(NDC):
                ww = w if dc == 0 else []
                ent("pe", (lambda t, fc=fc, dc=dc: t.matmul(
                    FB[fc % 4][:, 0:256],
                    wfc16[:, dc * FF + fc * P: dc * FF + (fc + 1) * P],
                    x1b[:, dc * TTOK:(dc + 1) * TTOK],
                    start=(dc == 0), stop=(dc == NDC - 1))),
                    ev=(f"f1({tau},{fc})" if dc == NDC - 1 else None),
                    waits=ww)

        def act_relu(tau, fc):
            ent("ac", (lambda a, fc=fc: a.activation(
                out=rr[fc % 6][:], in_=FB[fc % 4][:, 0:256], func=AF.Relu)),
                ev=f"relu({tau},{fc})",
                waits=[f"f1({tau},{fc})"] + ([f"h1m({tau},{fc - 6})"] if fc >= 6 else []))

        def dve_h1m(tau, fc):
            ent("ve", (lambda v, fc=fc: v.tensor_tensor(
                out=h1[:, fc * TTOK:(fc + 1) * TTOK], in0=rr[fc % 6][:],
                in1=rr[fc % 6][:], op=ALU.mult)),
                ev=f"h1m({tau},{fc})", waits=[f"relu({tau},{fc})"])

        def pe_ffn2_dc(tau, dc):
            w = []
            if dc == 0:
                w += [f"h1m({tau},{NFC - 4})", "dw_wp7"]
            if dc < 4:
                w.append(f"relu({tau},{NFC - 4 + dc})")
            else:
                w.append(f"vev({tau},{dc - 4})")
            for fc in range(NFC):
                ww = w if fc == 0 else (
                    [f"h1m({tau},{NFC - 1})"] if (dc == 0 and fc == NFC - 3)
                    else [])
                ent("pe", (lambda t, dc=dc, fc=fc: t.matmul(
                    FB[dc % 4][:, 0:256],
                    wpj16[:, fc * DM + dc * P: fc * DM + (dc + 1) * P],
                    h1[:, fc * TTOK:(fc + 1) * TTOK],
                    start=(fc == 0), stop=(fc == NFC - 1))),
                    ev=(f"f2({tau},{dc})" if fc == NFC - 1 else None),
                    waits=ww)

        def act_vev(tau, dc):
            w = [f"f2({tau},{dc})"]
            if dc >= 2:
                w.append(f"spk({tau},{dc - 2})")
            elif tau > 0:
                w.append(f"outd_tot({(tau - 1) * NDC + 6 + (dc % 2)})")
            ent("ac", (lambda a, dc=dc: a.activation(
                out=vsb[dc % 2][:], in_=FB[dc % 4][:, 0:256], func=AF.Copy)),
                ev=f"vev({tau},{dc})", waits=w)

        def dve_spike(tau, dc):
            sl = slice(dc * TTOK, (dc + 1) * TTOK)
            b = dc % 2
            w = [f"vev({tau},{dc})"]
            ent("ve", (lambda v, b=b: v.tensor_scalar(
                out=sacc[b][:], in0=vsb[b][:], scalar1=B_SPIKE[0], scalar2=None,
                op0=ALU.is_gt)), waits=w)
            for k in range(1, 5):
                ent("ve", (lambda v, b=b, k=k: v.scalar_tensor_tensor(
                    out=sacc[b][:], in0=vsb[b][:], scalar=B_SPIKE[k],
                    in1=sacc[b][:], op0=ALU.is_gt, op1=ALU.add)))
            xw = x1[tau % 2]
            ent("ve", (lambda v, b=b, sl=sl, xw=xw: v.scalar_tensor_tensor(
                out=r2_sb[:, sl], in0=sacc[b][:], scalar=0.2, in1=xw[:, sl],
                op0=ALU.mult, op1=ALU.add)), ev=f"spk({tau},{dc})")

        def act_r2casts(tau, dcs):
            for dc in dcs:
                sl = slice(dc * TTOK, (dc + 1) * TTOK)
                w = [f"spk({tau},{dc})"]
                if dc == 0 and tau > 0:
                    w.append(f"st2mm({tau - 1},{NDC - 1})")
                ent("ac", (lambda a, sl=sl: a.activation(
                    out=r2c16f[:, sl], in_=r2_sb[:, sl], func=AF.Copy)),
                    ev=f"r2cc({tau},{dc})", waits=w)
                ent("ac", (lambda a, sl=sl: a.activation(
                    out=r2sq16f[:, sl], in_=r2_sb[:, sl], func=AF.Square)),
                    ev=f"r2c({tau},{dc})")

        def pe_stats2(tau):
            for dc in range(NDC):
                sl = slice(dc * TTOK, (dc + 1) * TTOK)
                w = [f"r2c({tau},{dc})", f"r2cc({tau},{dc})"]
                if dc == 0 and tau > 0:
                    w.append(f"mq2({tau - 1})")
                ent("pe", (lambda t, sl=sl, dc=dc: t.matmul(
                    pb_s1[0:1, 256:512], ones_c[:], r2c16f[:, sl],
                    start=(dc == 0), stop=(dc == NDC - 1))), waits=w)
                ent("pe", (lambda t, sl=sl, dc=dc: t.matmul(
                    pb_s2[0:1, 256:512], ones_c[:], r2sq16f[:, sl],
                    start=(dc == 0), stop=(dc == NDC - 1))),
                    ev=f"st2mm({tau},{dc})")

        def dve_ln2_apply(tau):
            for dc in range(NDC):
                sl = slice(dc * TTOK, (dc + 1) * TTOK)
                b = dc % 2
                w = [f"bc2({tau})"] if dc == 0 else []
                nprev = tau * NDC + dc - 2
                if nprev >= 0:
                    w.append(f"outd_tot({nprev})")
                ent("ve", (lambda v, sl=sl, b=b: v.tensor_tensor(
                    out=out_st[b][:], in0=r2_sb[:, sl], in1=pb_b2[:, 0:256],
                    op=ALU.subtract)), waits=w)
                ent("ve", (lambda v, b=b: v.tensor_tensor(
                    out=out_st[b][:], in0=out_st[b][:], in1=pb_b2[:, 256:512],
                    op=ALU.mult)), ev=f"ln2({tau},{dc})")
                ent("sy", (lambda s, tau=tau, dc=dc, b=b: s.dma_start(
                    out=out_d[dc * P:(dc + 1) * P, tau * TTOK:(tau + 1) * TTOK],
                    in_=out_st[b][:])),
                    ev=f"outd({tau},{dc})", dma_sem=f"o{b}",
                    waits=[f"ln2({tau},{dc})"])

        def sync_xs(tau, dcs, first=False):
            for dc in dcs:
                if dc >= 2:
                    w = [f"epi({tau},{dc - 2})"]
                elif tau == 0:
                    w = ["scan"]
                else:
                    w = [f"epi({tau - 1},{dc + 6})"]
                src = xT.rearrange("(c p) t -> p c t", p=P)[
                    :, dc, tau * TTOK:(tau + 1) * TTOK]
                ent("sy", (lambda s, src=src, dc=dc: s.dma_start(
                    out=xs_d[dc % 2][:], in_=src)),
                    ev=f"xsd({tau},{dc})", dma_sem=f"x{dc % 2}", waits=w)

        FB = [pbf[0], pbf[1], pby[0], pby[1]]
        YB = [pb_s1, pb_s2]
        nc._sched = sched
        pe_y(0, range(0, 4))
        dve_epi(0, range(0, 4))
        pe_y(0, range(4, NDC))
        dve_epi(0, range(4, NDC))
        act_rcasts(0)
        pe_stats1(0)
        math_ln(0, 1)
        pe_bcast(0, 1)
        dve_ln1_apply(0)
        act_x1b(0)
        if not SAFE:
            for q in range(2, NDC):
                ent("pe", (lambda t, q=q: t.matmul(
                    pb_b2[:, 0:256], cw16[:, 0:P], cw16[:, 256:512],
                    start=True, stop=True)),
                    waits=([f"dw_wfc{q}"] if q < NDC - 1 else ["dw_wfc"]))

        for tau in range(NT):
            nx = tau + 1
            for fc in range(NFC):
                pe_ffn1_fc(tau, fc)
                act_relu(tau, fc)
                dve_h1m(tau, fc)
                if tau > 0 and fc == 2:
                    pe_stats2(tau - 1)
                if tau > 0 and fc == 3:
                    math_ln_a(tau - 1, 2)
                    math_ln_b(tau - 1, 2)
                if tau > 0 and fc == 5:
                    math_ln_c(tau - 1, 2)
                if nx < NT and fc == 2:
                    sync_xs(nx, range(0, 2))
                if nx < NT and fc == 4:
                    pe_y(nx, range(0, 2))
                if nx < NT and fc == 6:
                    pe_y(nx, range(2, 4))
                if nx < NT and fc == 7:
                    dve_epi(nx, range(0, 2))
                    sync_xs(nx, range(2, 4))
                if nx < NT and fc == 9:
                    pe_y(nx, range(4, NDC))
                if tau > 0 and fc == 10:
                    pe_bcast(tau - 1, 2)
                if nx < NT and fc == 11:
                    dve_epi(nx, range(2, 4))
                    sync_xs(nx, range(4, 6))
                if nx < NT and fc == 15:
                    dve_epi(nx, range(4, 6))
                    sync_xs(nx, range(6, NDC))
                if nx < NT and fc == 19:
                    dve_epi(nx, range(6, NDC))


            if tau > 0:
                dve_ln2_apply(tau - 1)
            pe_ffn2_dc(tau, 0)
            if nx < NT:
                act_rcasts(nx)
            act_vev(tau, 0)
            if nx < NT:
                pe_stats1(nx, range(0, 4))
            pe_ffn2_dc(tau, 1)
            act_vev(tau, 1)
            if nx < NT:
                pe_stats1(nx, range(4, NDC))
                math_ln_a(nx, 1)
            pe_ffn2_dc(tau, 2)
            dve_spike(tau, 0)
            if nx < NT:
                math_ln_b(nx, 1)
                math_ln_c(nx, 1)
            act_vev(tau, 2)
            pe_ffn2_dc(tau, 3)
            dve_spike(tau, 1)
            act_vev(tau, 3)
            if nx < NT:
                pe_bcast(nx, 1)
            pe_ffn2_dc(tau, 4)
            dve_spike(tau, 2)
            act_vev(tau, 4)
            if nx < NT:
                dve_ln1_apply(nx, range(0, 4))
                act_x1b(nx, range(0, 4))
            pe_ffn2_dc(tau, 5)
            dve_spike(tau, 3)
            act_vev(tau, 5)
            if nx < NT:
                dve_ln1_apply(nx, range(4, NDC))
                act_x1b(nx, range(4, NDC))
            pe_ffn2_dc(tau, 6)
            dve_spike(tau, 4)
            act_vev(tau, 6)
            act_r2casts(tau, range(0, 2))
            pe_ffn2_dc(tau, 7)
            dve_spike(tau, 5)
            act_vev(tau, 7)
            act_r2casts(tau, range(2, 4))
            dve_spike(tau, 6)
            act_r2casts(tau, range(4, 6))
            dve_spike(tau, 7)
            act_r2casts(tau, range(6, NDC))

        pe_stats2(NT - 1)
        math_ln(NT - 1, 2)
        pe_bcast(NT - 1, 2)
        dve_ln2_apply(NT - 1)

        counters = {k: 0 for k in SEMS}
        events = {}
        own = {"ve": 0, "ac": 0}
        for e in sched:
            eng = e["engine"]
            if e["dma_sem"] is not None:
                counters[e["dma_sem"]] += 16
                if e["ev"]:
                    events[e["ev"]] = (e["dma_sem"], counters[e["dma_sem"]])
            elif eng == "pe":
                if e["ev"]:
                    counters["pe"] += 1
                    assert e["ev"] not in events, f"dup event {e['ev']}"
                    events[e["ev"]] = ("pe", counters["pe"])
            elif eng in own or eng == "gp":
                semkey = "gp" if eng == "gp" else eng
                counters[semkey] += 1
                if e["ev"]:
                    events[e["ev"]] = (semkey, counters[semkey])
            else:
                assert e["ev"] is None, e["ev"]
        par_count = {0: 0, 1: 0}
        for tau in range(NT):
            for dc in range(NDC):
                b = dc % 2
                par_count[b] += 16
                events[f"outd_tot({tau * NDC + dc})"] = (f"o{b}", par_count[b])

        needed = {}
        for e in sched:
            for wname in e["waits"]:
                semkey, val = events[wname]
                needed.setdefault(semkey, set()).add(val)

        block = ctx.enter_context(nc.Block())

        def emit_for(engine_key, eng):
            own_count = 0
            pending = [0]
            pe_count = [0]
            wm = {}
            for e in sched:
                if e["engine"] != engine_key:
                    continue
                for wname in e["waits"]:
                    semkey, val = events[wname]
                    if wm.get(semkey, -1) >= val:
                        continue
                    wm[semkey] = val
                    eng.wait_ge(SEMS[semkey], val)
                if SAFE and engine_key in ("ve", "ac", "gp") and own_count > 0:
                    eng.wait_ge(SEMS[engine_key], own_count)
                inst = e["build"](eng)
                if e["dma_sem"] is not None:
                    inst.then_inc(SEMS[e["dma_sem"]], 16)
                elif engine_key in ("ve", "ac", "gp"):
                    own_count += 1
                    inst.then_inc(SEMS[engine_key], 1)
                elif engine_key == "pe" and e["ev"]:
                    inst.then_inc(SEMS["pe"], 1)

        @block.tensor
        def _(t):
            emit_for("pe", t)

        @block.vector
        def _(v):
            emit_for("ve", v)

        @block.scalar
        def _(a):
            emit_for("ac", a)

        @block.gpsimd
        def _(g):
            emit_for("gp", g)

        @block.sync
        def _(s):
            emit_for("sy", s)
            s.wait_ge(SEMS["o0"], par_count[0])
            s.wait_ge(SEMS["o1"], par_count[1])

    return nc


_NC_CACHE = None


def kernel(L0_attn_x, L0_attn_B_w, L0_attn_C_w, L0_attn_log_A, L0_attn_D_param,
           L0_ffn_W_fc, L0_ffn_W_proj):
    global _NC_CACHE
    if _NC_CACHE is None:
        _NC_CACHE = build_nc()
    nc = _NC_CACHE

    x = np.asarray(L0_attn_x)
    bwT = np.ascontiguousarray(np.asarray(L0_attn_B_w).T)
    cwT = np.ascontiguousarray(np.asarray(L0_attn_C_w).T)
    logA = np.ascontiguousarray(np.asarray(L0_attn_log_A).reshape(P, 1))
    dre = np.ascontiguousarray(np.asarray(L0_attn_D_param).reshape(NDC, P).T)
    wfcT = np.ascontiguousarray(np.asarray(L0_ffn_W_fc).T)
    wprojT = np.ascontiguousarray(np.asarray(L0_ffn_W_proj).T)

    in_maps = []
    for c in range(8):
        in_maps.append({
            "xT": np.ascontiguousarray(x[c].T),
            "bwT": bwT, "cwT": cwT, "logA": logA, "dre": dre,
            "wfcT": wfcT, "wprojT": wprojT,
        })
    res = run_bass_kernel_spmd(nc, in_maps, core_ids=list(range(8)))
    out = np.stack([np.asarray(r["out"]).T for r in res.results], axis=0)
    return np.ascontiguousarray(out.astype(np.float32))


# version 29
# speedup vs baseline: 1.0426x; 1.0426x over previous
import numpy as np
from contextlib import ExitStack

import concourse.bass as bass
import concourse.mybir as mybir
from concourse.bass_utils import run_bass_kernel_spmd

F32 = mybir.dt.float32
BF16 = mybir.dt.bfloat16
AF = mybir.ActivationFunctionType
ALU = mybir.AluOpType

SAFE = False

P = 128
L = 2048
DM = 1024
NDC = DM // P
FF = 4096
NFC = FF // P
TTOK = 256
NT = L // TTOK
EPS = 1e-5

B_SPIKE = [10000.0 / 40951.0, 19000.0 / 40951.0, 28100.0 / 40951.0,
           37100.0 / 40951.0, 44390.0 / 40951.0]


def _sched_entry(engine, build, ev=None, waits=(), dma_sem=None):
    return {"engine": engine, "build": build, "ev": ev, "waits": tuple(waits),
            "dma_sem": dma_sem}


def build_nc():
    nc = bass.Bass("TRN2", target_bir_lowering=False,
                   dynamic_dma_scratch_size=8192,
                   detect_race_conditions=SAFE)

    xT = nc.dram_tensor("xT", [DM, L], F32, kind="ExternalInput").ap()
    bwT = nc.dram_tensor("bwT", [DM, P], F32, kind="ExternalInput").ap()
    cwT = nc.dram_tensor("cwT", [P, DM], F32, kind="ExternalInput").ap()
    logA = nc.dram_tensor("logA", [P, 1], F32, kind="ExternalInput").ap()
    dre = nc.dram_tensor("dre", [P, NDC], F32, kind="ExternalInput").ap()
    wfcT = nc.dram_tensor("wfcT", [DM, FF], F32, kind="ExternalInput").ap()
    wprojT = nc.dram_tensor("wprojT", [FF, DM], F32, kind="ExternalInput").ap()
    out_d = nc.dram_tensor("out", [DM, L], F32, kind="ExternalOutput").ap()

    with ExitStack() as ctx:
        def sb(name, shape, dt):
            return ctx.enter_context(nc.sbuf_tensor(name, shape, dt))

        def psum(name):
            return ctx.enter_context(nc.psum_tensor(name, [P, 512], F32))

        def sem(name):
            return ctx.enter_context(nc.semaphore(name))

        wfc16 = sb("wfc16", [P, NDC * FF], BF16)
        wpj16 = sb("wpj16", [P, NFC * DM], BF16)
        bw16 = sb("bw16", [P, DM], BF16)
        cw16 = sb("cw16", [P, DM], BF16)
        logA_sb = sb("logA_sb", [P, 1], F32)
        acol = sb("acol", [P, 1], F32)
        dre_sb = sb("dre_sb", [P, NDC], F32)
        d1p = sb("d1p", [P, NDC], F32)
        ones_c = sb("ones_c", [P, 1], BF16)
        eps_c = sb("eps_c", [P, 1], F32)
        ones_r = sb("ones_r", [1, P], BF16)
        arena16 = sb("arena16", [P, 16384], BF16)
        xb16 = arena16[:, 0:16384]
        h1 = arena16[:, 0:8192]
        r2c16f = arena16[:, 8192:10240]
        r2sq16f = arena16[:, 10240:12288]
        r16d = [arena16[:, 12288 + i * 256:12288 + (i + 1) * 256]
                for i in range(2)] + \
               [arena16[:, 11264 + i * 256:11264 + (i + 1) * 256]
                for i in range(2)]
        rsq16d = [arena16[:, 12800 + i * 256:12800 + (i + 1) * 256]
                  for i in range(2)] + \
                 [arena16[:, 11776 + i * 256:11776 + (i + 1) * 256]
                  for i in range(2)]
        rr = [arena16[:, 13312 + i * 256:13312 + (i + 1) * 256]
              for i in range(4)] + [
              arena16[:, 15360 + i * 256:15360 + (i + 1) * 256]
              for i in range(2)]
        mi16 = [arena16[0:1, 14336 + i * 512:14336 + (i + 1) * 512]
                for i in range(2)]
        arena32 = sb("arena32", [P, 2816], F32)
        xp = arena32[:, 0:2048]
        vsb = [arena32[:, 0 + i * 256:0 + (i + 1) * 256] for i in range(2)]
        sacc = [arena32[:, 512 + i * 256:512 + (i + 1) * 256] for i in range(2)]
        xs_d = [arena32[:, 1024 + i * 256:1024 + (i + 1) * 256] for i in range(2)]
        stA = [arena32[0:1, 1536 + i * 256:1536 + (i + 1) * 256] for i in range(2)]
        stB = [arena32[0:1, 2048 + i * 256:2048 + (i + 1) * 256] for i in range(2)]
        tmp_row = arena32[0:1, 2560:2816]
        out_st = xs_d
        hs16 = sb("hs16", [P, L], BF16)
        r_sb = sb("r_sb", [P, NDC * TTOK], F32)
        r2_sb = sb("r2_sb", [P, NDC * TTOK], F32)
        x1 = [sb("x1_0", [P, NDC * TTOK], F32),
              sb("x1_1", [P, NDC * TTOK], F32)]
        x1b = sb("x1b", [P, NDC * TTOK], BF16)

        pbf = [psum("pbf0"), psum("pbf1")]
        pby = [psum("pby0"), psum("pby1")]
        pb_s1 = psum("pb_s1")
        pb_s2 = psum("pb_s2")
        pb_b1 = psum("pb_b1")
        pb_b2 = psum("pb_b2")

        s_pe = sem("s_pe")
        s_gp = sem("s_gp")
        s_ve = sem("s_ve")
        s_ac = sem("s_ac")
        s_dw1 = sem("s_dw1")
        s_dwf = sem("s_dwf")
        s_dwp = sem("s_dwp")
        s_c = sem("s_c")
        s_x0 = sem("s_x0")
        s_x1 = sem("s_x1")
        s_o = [sem("s_o0"), sem("s_o1")]

        SEMS = {"pe": s_pe, "gp": s_gp, "ve": s_ve, "ac": s_ac, "dw1": s_dw1,
                "dwf": s_dwf, "dwp": s_dwp, "c": s_c, "x0": s_x0,
                "x1": s_x1, "o0": s_o[0], "o1": s_o[1]}

        sched = []

        ent_ctx = {"label": ""}

        def ent(engine, build, ev=None, waits=(), dma_sem=None):
            e = _sched_entry(engine, build, ev, waits, dma_sem)
            e["label"] = ev or ent_ctx["label"]
            sched.append(e)

        ent("gp", lambda g: g.dma_start(
            out=bw16[:].rearrange("p (c n) -> p c n", c=NDC),
            in_=bwT.rearrange("(c p) n -> p c n", p=P)),
            ev="dw_bw", dma_sem="dw1")
        ent("gp", lambda g: g.dma_start(out=cw16[:], in_=cwT),
            ev="dw_cw", dma_sem="dw1")
        ent("gp", lambda g: g.dma_start(
            out=xb16.rearrange("p (c t) -> p c t", c=NDC),
            in_=xT.rearrange("(c p) t -> p c t", p=P)),
            ev="dw_xb", dma_sem="dw1")
        for q in range(NDC):
            src = wfcT.rearrange("(c p) f -> p c f", p=P)[:, q:q + 1, :]
            dst = wfc16[:, q * FF:(q + 1) * FF]
            ent("gp", (lambda g, s=src, d=dst: g.dma_start(
                out=d.rearrange("p (c f) -> p c f", c=1), in_=s)),
                ev=(f"dw_wfc" if q == NDC - 1 else f"dw_wfc{q}"), dma_sem="dwf")
        for q in range(8):
            qs = NFC // 8
            src = wprojT.rearrange("(c p) d -> p c d", p=P)[:, q * qs:(q + 1) * qs, :]
            dst = wpj16[:, q * qs * DM:(q + 1) * qs * DM]
            ent("gp", (lambda g, s=src, d=dst: g.dma_start(out=d, in_=s)),
                ev=f"dw_wp{q}", dma_sem="dwp")
        ent("sy", lambda s: s.dma_start(out=logA_sb[:], in_=logA),
            ev="c_logA", dma_sem="c")
        ent("sy", lambda s: s.dma_start(out=dre_sb[:], in_=dre),
            ev="c_dre", dma_sem="c")
        ent("sy", lambda s: s.dma_start(
            out=r_sb[:].rearrange("p (c t) -> p c t", t=TTOK),
            in_=xT.rearrange("(c p) t -> p c t", p=P)[:, :, 0:TTOK]),
            ev="c_xr0", dma_sem="c")


        ent("ve", lambda v: v.memset(ones_c[:], 1.0), ev="ones_c")
        ent("ve", lambda v: v.memset(eps_c[:], EPS), ev="eps_c")
        ent("ve", lambda v: v.memset(ones_r[:], 1.0), ev="ones_r")
        ent("ac", lambda a: a.activation(out=acol[:], in_=logA_sb[:], func=AF.Exp),
            ev="A", waits=["c_xr0"])
        ent("ac", lambda a: a.activation(out=d1p[:], in_=dre_sb[:],
                                         func=AF.Identity, bias=1.0),
            ev="D1", waits=["c_xr0"])

        for tq in range(4):
            waits = ["dw_xb"] if tq == 0 else [f"xpev{tq - 2}"] if tq >= 2 else []
            for dc in range(NDC):
                w = list(waits) if dc == 0 else []
                ent("pe", (lambda t, tq=tq, dc=dc: t.matmul(
                    pbf[tq % 2][:, 0:512],
                    bw16[:, dc * P:(dc + 1) * P],
                    xb16[:, dc * L + tq * 512: dc * L + (tq + 1) * 512],
                    start=(dc == 0), stop=(dc == NDC - 1))),
                    ev=(f"xpmm{tq}" if dc == NDC - 1 else None), waits=w)
            ent("ac", (lambda a, tq=tq: a.activation(
                out=xp[:, tq * 512:(tq + 1) * 512], in_=pbf[tq % 2][:, 0:512],
                func=AF.Copy)), ev=f"xpev{tq}", waits=[f"xpmm{tq}"])


        ent("ve", lambda v: v.tensor_tensor_scan(
            out=hs16[:], data0=acol[:].broadcast_to([P, L]), data1=xp[:],
            initial=0.0, op0=ALU.mult, op1=ALU.add),
            ev="scan", waits=["xpev3", "A"])


        def pe_y(tau, dcs=range(NDC)):
            for dc in dcs:
                slot = dc if dc < 6 else dc - 6
                w = []
                if tau == 0 and dc == 0:
                    w.append("scan")
                if dc >= 6:
                    w.append(f"epi({tau},{dc - 6})")
                elif tau > 0:
                    w.append(f"epi({tau - 1},{dc + 6})" if dc < 2 else None)
                if slot < 4:
                    half = (slot // 2) % 2
                    if tau > 0 and half == 0:
                        w.append(f"mq1({tau - 1})")
                    if tau > 1 and half == 1:
                        w.append(f"mq2({tau - 2})")
                    dst = YB[slot % 2][:, half * 256:half * 256 + 256]
                else:
                    if tau > 0:
                        w.append(f"ln1({tau - 1},{NDC - 1})")
                    dst = pb_b1[:, (slot - 4) * 256:(slot - 4) * 256 + 256]
                w = [x for x in w if x]
                ent("pe", (lambda t, tau=tau, dc=dc, dst=dst: t.matmul(
                    dst,
                    cw16[:, dc * P:(dc + 1) * P],
                    hs16[:, tau * TTOK:(tau + 1) * TTOK],
                    start=True, stop=True)),
                    ev=f"y({tau},{dc})", waits=w)

        def dve_epi(tau, dcs=range(NDC)):
            for dc in dcs:
                slot = dc if dc < 6 else dc - 6
                if slot < 4:
                    half = (slot // 2) % 2
                    src = YB[slot % 2][:, half * 256:half * 256 + 256]
                else:
                    src = pb_b1[:, (slot - 4) * 256:(slot - 4) * 256 + 256]
                sl = slice(dc * TTOK, (dc + 1) * TTOK)
                if tau == 0:
                    w = [f"y({tau},{dc})", "D1", "c_xr0"]
                    ent("ve", (lambda v, dc=dc, sl=sl: v.tensor_scalar(
                        out=r_sb[:, sl], in0=r_sb[:, sl],
                        scalar1=d1p[:, dc:dc + 1], scalar2=None, op0=ALU.mult)),
                        waits=w)
                else:
                    w = [f"y({tau},{dc})", "D1", f"xsd({tau},{dc})"]
                    ent("ve", (lambda v, dc=dc, sl=sl: v.tensor_scalar(
                        out=r_sb[:, sl], in0=xs_d[dc % 2][:],
                        scalar1=d1p[:, dc:dc + 1], scalar2=None, op0=ALU.mult)),
                        waits=w)
                ent("ve", (lambda v, sl=sl, src=src: v.tensor_tensor(
                    out=r_sb[:, sl], in0=r_sb[:, sl],
                    in1=src, op=ALU.add)),
                    ev=f"epi({tau},{dc})")

        def act_rcasts(tau, dcs=range(NDC)):
            for dc in dcs:
                sl = slice(dc * TTOK, (dc + 1) * TTOK)
                w = [f"epi({tau},{dc})"]
                if tau > 0 or dc >= 4:
                    w.append(f"st1mm({tau},{dc - 4})" if dc >= 4
                             else f"st1mm({tau - 1},{dc + 4})")
                ent("ac", (lambda a, dc=dc, sl=sl: a.activation(
                    out=r16d[dc % 4][:], in_=r_sb[:, sl], func=AF.Copy)),
                    waits=w)
                ent("ac", (lambda a, dc=dc, sl=sl: a.activation(
                    out=rsq16d[dc % 4][:], in_=r_sb[:, sl], func=AF.Square)),
                    ev=f"r16({tau},{dc})")

        def pe_stats1(tau, dcs=range(NDC)):
            for dc in dcs:
                w = [f"r16({tau},{dc})"]
                if dc == 0:
                    w.append(f"epi({tau},{NDC - 1})")
                    w.append(f"mq1({tau - 1})" if tau > 0 else None)
                w = [x for x in w if x]
                ent("pe", (lambda t, dc=dc: t.matmul(
                    pb_s1[0:1, 0:256], ones_c[:], r16d[dc % 4][:],
                    start=(dc == 0), stop=(dc == NDC - 1))), waits=w)
                ent("pe", (lambda t, dc=dc: t.matmul(
                    pb_s2[0:1, 0:256], ones_c[:], rsq16d[dc % 4][:],
                    start=(dc == 0), stop=(dc == NDC - 1))),
                    ev=f"st1mm({tau},{dc})")

        def math_ln_a(tau, which):
            off = 0 if which == 1 else 256
            i = which - 1
            suf = f"{which}({tau})"
            ent("ac", (lambda a: a.activation(
                out=stA[i][:], in_=pb_s1[0:1, off:off + 256], func=AF.Copy,
                scale=1.0 / DM)), waits=[f"st{which}mm({tau},{NDC - 1})"])
            ent("ac", (lambda a: a.activation(
                out=mi16[i][0:1, 0:TTOK], in_=stA[i][:], func=AF.Copy)))
            ent("ac", (lambda a: a.activation(
                out=stB[i][:], in_=pb_s2[0:1, off:off + 256], func=AF.Copy,
                scale=1.0 / DM)), ev=f"mq{suf}")

        def math_ln_b(tau, which):
            i = which - 1
            suf = f"{which}({tau})"
            ent("ve", (lambda v, i=i: v.tensor_tensor(
                out=tmp_row[:], in0=stA[i][:], in1=stA[i][:], op=ALU.mult)),
                waits=[f"mq{suf}"])
            ent("ve", (lambda v, i=i: v.tensor_tensor(
                out=stB[i][:], in0=stB[i][:], in1=tmp_row[:], op=ALU.subtract)),
                ev=f"var{suf}")

        def math_ln_c(tau, which):
            i = which - 1
            suf = f"{which}({tau})"
            ent("ac", (lambda a, i=i: a.activation(
                out=tmp_row[:], in_=stB[i][:], func=AF.Ln, bias=eps_c[0:1, :])),
                waits=[f"var{suf}"])
            ent("ac", (lambda a, i=i: a.activation(
                out=stA[i][:], in_=tmp_row[:], func=AF.Exp, scale=-0.5)))
            ent("ac", (lambda a, i=i: a.activation(
                out=mi16[i][0:1, TTOK:2 * TTOK], in_=stA[i][:], func=AF.Copy)),
                ev=f"lm{suf}")

        def math_ln(tau, which):
            math_ln_a(tau, which)
            math_ln_b(tau, which)
            math_ln_c(tau, which)

        def pe_bcast(tau, which):
            pb = pb_b1 if which == 1 else pb_b2
            i = which - 1
            w = [f"lm{which}({tau})"]
            if which == 1 and tau > 0:
                w.append(f"ln1({tau - 1},{NDC - 1})")
            if which == 1:
                w.append(f"epi({tau},5)")
            if which == 2 and tau > 0:
                w.append(f"ln2({tau - 1},{NDC - 1})")
            ent("pe", (lambda t: t.matmul(
                pb[:, 0:256], ones_r[:], mi16[i][0:1, 0:TTOK],
                start=True, stop=True)), waits=w)
            ent("pe", (lambda t: t.matmul(
                pb[:, 256:512], ones_r[:], mi16[i][0:1, TTOK:2 * TTOK],
                start=True, stop=True)), ev=f"bc{which}({tau})")

        def dve_ln1_apply(tau, dcs=range(NDC)):
            xw = x1[tau % 2]
            for dc in dcs:
                sl = slice(dc * TTOK, (dc + 1) * TTOK)
                w = [f"bc1({tau})"] if dc == dcs[0] else []
                ent("ve", (lambda v, sl=sl, xw=xw: v.tensor_tensor(
                    out=xw[:, sl], in0=r_sb[:, sl], in1=pb_b1[:, 0:256],
                    op=ALU.subtract)), waits=w)
                ent("ve", (lambda v, sl=sl, xw=xw: v.tensor_tensor(
                    out=xw[:, sl], in0=xw[:, sl], in1=pb_b1[:, 256:512],
                    op=ALU.mult)), ev=f"ln1({tau},{dc})")

        def act_x1b(tau, dcs=range(NDC)):
            xw = x1[tau % 2]
            for dc in dcs:
                sl = slice(dc * TTOK, (dc + 1) * TTOK)
                ent("ac", (lambda a, sl=sl, xw=xw: a.activation(
                    out=x1b[:, sl], in_=xw[:, sl], func=AF.Copy)),
                    ev=f"x1b({tau},{dc})", waits=[f"ln1({tau},{dc})"])

        def pe_ffn1_fc(tau, fc):
            w = []
            if fc == 0:
                w += [f"x1b({tau},{NDC - 1})", "dw_wfc"]
            if fc < 4 and tau > 0:
                w.append(f"vev({tau - 1},{fc + 4})")
            elif fc >= 4:
                w.append(f"relu({tau},{fc - 4})")
            for dc in range(NDC):
                ww = w if dc == 0 else []
                ent("pe", (lambda t, fc=fc, dc=dc: t.matmul(
                    FB[fc % 4][:, 0:256],
                    wfc16[:, dc * FF + fc * P: dc * FF + (fc + 1) * P],
                    x1b[:, dc * TTOK:(dc + 1) * TTOK],
                    start=(dc == 0), stop=(dc == NDC - 1))),
                    ev=(f"f1({tau},{fc})" if dc == NDC - 1 else None),
                    waits=ww)

        def act_relu(tau, fc):
            ent("ac", (lambda a, fc=fc: a.activation(
                out=rr[fc % 6][:], in_=FB[fc % 4][:, 0:256], func=AF.Relu)),
                ev=f"relu({tau},{fc})",
                waits=[f"f1({tau},{fc})"] + ([f"h1m({tau},{fc - 6})"] if fc >= 6 else []))

        def dve_h1m(tau, fc):
            ent("ve", (lambda v, fc=fc: v.tensor_tensor(
                out=h1[:, fc * TTOK:(fc + 1) * TTOK], in0=rr[fc % 6][:],
                in1=rr[fc % 6][:], op=ALU.mult)),
                ev=f"h1m({tau},{fc})", waits=[f"relu({tau},{fc})"])

        def pe_ffn2_dc(tau, dc):
            w = []
            if dc == 0:
                w += [f"h1m({tau},{NFC - 4})", "dw_wp7"]
            if dc < 4:
                w.append(f"relu({tau},{NFC - 4 + dc})")
            else:
                w.append(f"vev({tau},{dc - 4})")
            for fc in range(NFC):
                ww = w if fc == 0 else (
                    [f"h1m({tau},{NFC - 1})"] if (dc == 0 and fc == NFC - 3)
                    else [])
                ent("pe", (lambda t, dc=dc, fc=fc: t.matmul(
                    FB[dc % 4][:, 0:256],
                    wpj16[:, fc * DM + dc * P: fc * DM + (dc + 1) * P],
                    h1[:, fc * TTOK:(fc + 1) * TTOK],
                    start=(fc == 0), stop=(fc == NFC - 1))),
                    ev=(f"f2({tau},{dc})" if fc == NFC - 1 else None),
                    waits=ww)

        def act_vev(tau, dc):
            w = [f"f2({tau},{dc})"]
            if dc >= 2:
                w.append(f"spk({tau},{dc - 2})")
            ent("ac", (lambda a, dc=dc: a.activation(
                out=vsb[dc % 2][:], in_=FB[dc % 4][:, 0:256], func=AF.Copy)),
                ev=f"vev({tau},{dc})", waits=w)

        def dve_spike(tau, dc):
            sl = slice(dc * TTOK, (dc + 1) * TTOK)
            b = dc % 2
            w = [f"vev({tau},{dc})"]
            ent("ve", (lambda v, b=b: v.tensor_scalar(
                out=sacc[b][:], in0=vsb[b][:], scalar1=B_SPIKE[0], scalar2=None,
                op0=ALU.is_gt)), waits=w)
            for k in range(1, 5):
                ent("ve", (lambda v, b=b, k=k: v.scalar_tensor_tensor(
                    out=sacc[b][:], in0=vsb[b][:], scalar=B_SPIKE[k],
                    in1=sacc[b][:], op0=ALU.is_gt, op1=ALU.add)))
            xw = x1[tau % 2]
            ent("ve", (lambda v, b=b, sl=sl, xw=xw: v.scalar_tensor_tensor(
                out=r2_sb[:, sl], in0=sacc[b][:], scalar=0.2, in1=xw[:, sl],
                op0=ALU.mult, op1=ALU.add)), ev=f"spk({tau},{dc})")

        def act_r2casts(tau, dcs):
            for dc in dcs:
                sl = slice(dc * TTOK, (dc + 1) * TTOK)
                w = [f"spk({tau},{dc})"]
                if dc == 0 and tau > 0:
                    w.append(f"st2mm({tau - 1},{NDC - 1})")
                ent("ac", (lambda a, sl=sl: a.activation(
                    out=r2c16f[:, sl], in_=r2_sb[:, sl], func=AF.Copy)),
                    ev=f"r2cc({tau},{dc})", waits=w)
                ent("ac", (lambda a, sl=sl: a.activation(
                    out=r2sq16f[:, sl], in_=r2_sb[:, sl], func=AF.Square)),
                    ev=f"r2c({tau},{dc})")

        def pe_stats2(tau):
            for dc in range(NDC):
                sl = slice(dc * TTOK, (dc + 1) * TTOK)
                w = [f"r2c({tau},{dc})", f"r2cc({tau},{dc})"]
                if dc == 0 and tau > 0:
                    w.append(f"mq2({tau - 1})")
                ent("pe", (lambda t, sl=sl, dc=dc: t.matmul(
                    pb_s1[0:1, 256:512], ones_c[:], r2c16f[:, sl],
                    start=(dc == 0), stop=(dc == NDC - 1))), waits=w)
                ent("pe", (lambda t, sl=sl, dc=dc: t.matmul(
                    pb_s2[0:1, 256:512], ones_c[:], r2sq16f[:, sl],
                    start=(dc == 0), stop=(dc == NDC - 1))),
                    ev=f"st2mm({tau},{dc})")

        def dve_ln2_apply(tau):
            for dc in range(NDC):
                sl = slice(dc * TTOK, (dc + 1) * TTOK)
                b = dc % 2
                w = [f"bc2({tau})"] if dc == 0 else []
                nprev = tau * NDC + dc - 2
                if nprev >= 0:
                    w.append(f"outd_tot({nprev})")
                ent("ve", (lambda v, sl=sl, b=b: v.tensor_tensor(
                    out=out_st[b][:], in0=r2_sb[:, sl], in1=pb_b2[:, 0:256],
                    op=ALU.subtract)), waits=w)
                ent("ve", (lambda v, b=b: v.tensor_tensor(
                    out=out_st[b][:], in0=out_st[b][:], in1=pb_b2[:, 256:512],
                    op=ALU.mult)), ev=f"ln2({tau},{dc})")
                ent("sy", (lambda s, tau=tau, dc=dc, b=b: s.dma_start(
                    out=out_d[dc * P:(dc + 1) * P, tau * TTOK:(tau + 1) * TTOK],
                    in_=out_st[b][:])),
                    ev=f"outd({tau},{dc})", dma_sem=f"o{b}",
                    waits=[f"ln2({tau},{dc})"])

        def sync_xs(tau, dcs, first=False):
            for dc in dcs:
                if dc >= 2:
                    w = [f"epi({tau},{dc - 2})"]
                elif tau == 0:
                    w = ["scan"]
                else:
                    w = [f"epi({tau - 1},{dc + 6})"]
                    if tau >= 3:
                        w.append(f"outd_tot({(tau - 3) * NDC + 6 + dc})")
                src = xT.rearrange("(c p) t -> p c t", p=P)[
                    :, dc, tau * TTOK:(tau + 1) * TTOK]
                ent("sy", (lambda s, src=src, dc=dc: s.dma_start(
                    out=xs_d[dc % 2][:], in_=src)),
                    ev=f"xsd({tau},{dc})", dma_sem=f"x{dc % 2}", waits=w)

        FB = [pbf[0], pbf[1], pby[0], pby[1]]
        YB = [pb_s1, pb_s2]
        nc._sched = sched
        pe_y(0, range(0, 4))
        dve_epi(0, range(0, 4))
        pe_y(0, range(4, NDC))
        dve_epi(0, range(4, NDC))
        act_rcasts(0)
        pe_stats1(0)
        math_ln(0, 1)
        pe_bcast(0, 1)
        dve_ln1_apply(0)
        act_x1b(0)
        if not SAFE:
            for q in range(2, NDC):
                ent("pe", (lambda t, q=q: t.matmul(
                    pb_b2[:, 0:256], cw16[:, 0:P], cw16[:, 256:512],
                    start=True, stop=True)),
                    waits=([f"dw_wfc{q}"] if q < NDC - 1 else ["dw_wfc"]))

        for tau in range(NT):
            nx = tau + 1
            for fc in range(NFC):
                pe_ffn1_fc(tau, fc)
                act_relu(tau, fc)
                dve_h1m(tau, fc)
                if tau > 0 and fc == 2:
                    pe_stats2(tau - 1)
                if tau > 0 and fc == 3:
                    math_ln_a(tau - 1, 2)
                    math_ln_b(tau - 1, 2)
                if tau > 0 and fc == 5:
                    math_ln_c(tau - 1, 2)
                if nx < NT and fc == 2:
                    sync_xs(nx, range(0, 2))
                if nx < NT and fc == 4:
                    pe_y(nx, range(0, 2))
                if nx < NT and fc == 6:
                    pe_y(nx, range(2, 4))
                if nx < NT and fc == 7:
                    dve_epi(nx, range(0, 2))
                    sync_xs(nx, range(2, 4))
                if nx < NT and fc == 9:
                    pe_y(nx, range(4, NDC))
                if tau > 0 and fc == 10:
                    pe_bcast(tau - 1, 2)
                if nx < NT and fc == 11:
                    dve_epi(nx, range(2, 4))
                    sync_xs(nx, range(4, 6))
                if nx < NT and fc == 15:
                    dve_epi(nx, range(4, 6))
                    sync_xs(nx, range(6, NDC))
                if nx < NT and fc == 19:
                    dve_epi(nx, range(6, NDC))


            if tau > 0:
                dve_ln2_apply(tau - 1)
            pe_ffn2_dc(tau, 0)
            if nx < NT:
                act_rcasts(nx)
            act_vev(tau, 0)
            if nx < NT:
                pe_stats1(nx, range(0, 4))
            pe_ffn2_dc(tau, 1)
            act_vev(tau, 1)
            if nx < NT:
                pe_stats1(nx, range(4, NDC))
                math_ln_a(nx, 1)
            pe_ffn2_dc(tau, 2)
            dve_spike(tau, 0)
            if nx < NT:
                math_ln_b(nx, 1)
                math_ln_c(nx, 1)
            act_vev(tau, 2)
            pe_ffn2_dc(tau, 3)
            dve_spike(tau, 1)
            act_vev(tau, 3)
            if nx < NT:
                pe_bcast(nx, 1)
            pe_ffn2_dc(tau, 4)
            dve_spike(tau, 2)
            act_vev(tau, 4)
            if nx < NT:
                dve_ln1_apply(nx, range(0, 4))
                act_x1b(nx, range(0, 4))
            pe_ffn2_dc(tau, 5)
            dve_spike(tau, 3)
            act_vev(tau, 5)
            if nx < NT:
                dve_ln1_apply(nx, range(4, NDC))
                act_x1b(nx, range(4, NDC))
            pe_ffn2_dc(tau, 6)
            dve_spike(tau, 4)
            act_vev(tau, 6)
            act_r2casts(tau, range(0, 2))
            pe_ffn2_dc(tau, 7)
            dve_spike(tau, 5)
            act_vev(tau, 7)
            act_r2casts(tau, range(2, 4))
            dve_spike(tau, 6)
            act_r2casts(tau, range(4, 6))
            dve_spike(tau, 7)
            act_r2casts(tau, range(6, NDC))

        pe_stats2(NT - 1)
        math_ln(NT - 1, 2)
        pe_bcast(NT - 1, 2)
        dve_ln2_apply(NT - 1)

        counters = {k: 0 for k in SEMS}
        events = {}
        own = {"ve": 0, "ac": 0}
        for e in sched:
            eng = e["engine"]
            if e["dma_sem"] is not None:
                counters[e["dma_sem"]] += 16
                if e["ev"]:
                    events[e["ev"]] = (e["dma_sem"], counters[e["dma_sem"]])
            elif eng == "pe":
                if e["ev"]:
                    counters["pe"] += 1
                    assert e["ev"] not in events, f"dup event {e['ev']}"
                    events[e["ev"]] = ("pe", counters["pe"])
            elif eng in own or eng == "gp":
                semkey = "gp" if eng == "gp" else eng
                counters[semkey] += 1
                if e["ev"]:
                    events[e["ev"]] = (semkey, counters[semkey])
            else:
                assert e["ev"] is None, e["ev"]
        par_count = {0: 0, 1: 0}
        for tau in range(NT):
            for dc in range(NDC):
                b = dc % 2
                par_count[b] += 16
                events[f"outd_tot({tau * NDC + dc})"] = (f"o{b}", par_count[b])

        needed = {}
        for e in sched:
            for wname in e["waits"]:
                semkey, val = events[wname]
                needed.setdefault(semkey, set()).add(val)

        block = ctx.enter_context(nc.Block())

        def emit_for(engine_key, eng):
            own_count = 0
            pending = [0]
            pe_count = [0]
            wm = {}
            for e in sched:
                if e["engine"] != engine_key:
                    continue
                for wname in e["waits"]:
                    semkey, val = events[wname]
                    if wm.get(semkey, -1) >= val:
                        continue
                    wm[semkey] = val
                    eng.wait_ge(SEMS[semkey], val)
                if SAFE and engine_key in ("ve", "ac", "gp") and own_count > 0:
                    eng.wait_ge(SEMS[engine_key], own_count)
                inst = e["build"](eng)
                if e["dma_sem"] is not None:
                    inst.then_inc(SEMS[e["dma_sem"]], 16)
                elif engine_key in ("ve", "ac", "gp"):
                    own_count += 1
                    inst.then_inc(SEMS[engine_key], 1)
                elif engine_key == "pe" and e["ev"]:
                    inst.then_inc(SEMS["pe"], 1)

        @block.tensor
        def _(t):
            emit_for("pe", t)

        @block.vector
        def _(v):
            emit_for("ve", v)

        @block.scalar
        def _(a):
            emit_for("ac", a)

        @block.gpsimd
        def _(g):
            emit_for("gp", g)

        @block.sync
        def _(s):
            emit_for("sy", s)
            s.wait_ge(SEMS["o0"], par_count[0])
            s.wait_ge(SEMS["o1"], par_count[1])

    return nc


_NC_CACHE = None


def kernel(L0_attn_x, L0_attn_B_w, L0_attn_C_w, L0_attn_log_A, L0_attn_D_param,
           L0_ffn_W_fc, L0_ffn_W_proj):
    global _NC_CACHE
    if _NC_CACHE is None:
        _NC_CACHE = build_nc()
    nc = _NC_CACHE

    x = np.asarray(L0_attn_x)
    bwT = np.ascontiguousarray(np.asarray(L0_attn_B_w).T)
    cwT = np.ascontiguousarray(np.asarray(L0_attn_C_w).T)
    logA = np.ascontiguousarray(np.asarray(L0_attn_log_A).reshape(P, 1))
    dre = np.ascontiguousarray(np.asarray(L0_attn_D_param).reshape(NDC, P).T)
    wfcT = np.ascontiguousarray(np.asarray(L0_ffn_W_fc).T)
    wprojT = np.ascontiguousarray(np.asarray(L0_ffn_W_proj).T)

    in_maps = []
    for c in range(8):
        in_maps.append({
            "xT": np.ascontiguousarray(x[c].T),
            "bwT": bwT, "cwT": cwT, "logA": logA, "dre": dre,
            "wfcT": wfcT, "wprojT": wprojT,
        })
    res = run_bass_kernel_spmd(nc, in_maps, core_ids=list(range(8)))
    out = np.stack([np.asarray(r["out"]).T for r in res.results], axis=0)
    return np.ascontiguousarray(out.astype(np.float32))


# version 30
# speedup vs baseline: 1.0720x; 1.0282x over previous
import numpy as np
from contextlib import ExitStack

import concourse.bass as bass
import concourse.mybir as mybir
from concourse.bass_utils import run_bass_kernel_spmd

F32 = mybir.dt.float32
BF16 = mybir.dt.bfloat16
AF = mybir.ActivationFunctionType
ALU = mybir.AluOpType

SAFE = False

P = 128
L = 2048
DM = 1024
NDC = DM // P
FF = 4096
NFC = FF // P
TTOK = 256
NT = L // TTOK
EPS = 1e-5

B_SPIKE = [10000.0 / 40951.0, 19000.0 / 40951.0, 28100.0 / 40951.0,
           37100.0 / 40951.0, 44390.0 / 40951.0]


def _sched_entry(engine, build, ev=None, waits=(), dma_sem=None):
    return {"engine": engine, "build": build, "ev": ev, "waits": tuple(waits),
            "dma_sem": dma_sem}


def build_nc():
    nc = bass.Bass("TRN2", target_bir_lowering=False,
                   dynamic_dma_scratch_size=8192,
                   detect_race_conditions=SAFE)

    xT = nc.dram_tensor("xT", [DM, L], F32, kind="ExternalInput").ap()
    bwT = nc.dram_tensor("bwT", [DM, P], F32, kind="ExternalInput").ap()
    cwT = nc.dram_tensor("cwT", [P, DM], F32, kind="ExternalInput").ap()
    logA = nc.dram_tensor("logA", [P, 1], F32, kind="ExternalInput").ap()
    dre = nc.dram_tensor("dre", [P, NDC], F32, kind="ExternalInput").ap()
    wfcT = nc.dram_tensor("wfcT", [DM, FF], F32, kind="ExternalInput").ap()
    wprojT = nc.dram_tensor("wprojT", [FF, DM], F32, kind="ExternalInput").ap()
    out_d = nc.dram_tensor("out", [DM, L], F32, kind="ExternalOutput").ap()

    with ExitStack() as ctx:
        def sb(name, shape, dt):
            return ctx.enter_context(nc.sbuf_tensor(name, shape, dt))

        def psum(name):
            return ctx.enter_context(nc.psum_tensor(name, [P, 512], F32))

        def sem(name):
            return ctx.enter_context(nc.semaphore(name))

        wfc16 = sb("wfc16", [P, NDC * FF], BF16)
        wpj16 = sb("wpj16", [P, NFC * DM], BF16)
        bw16 = sb("bw16", [P, DM], BF16)
        cw16 = sb("cw16", [P, DM], BF16)
        logA_sb = sb("logA_sb", [P, 1], F32)
        acol = sb("acol", [P, 1], F32)
        dre_sb = sb("dre_sb", [P, NDC], F32)
        d1p = sb("d1p", [P, NDC], F32)
        ones_c = sb("ones_c", [P, 1], BF16)
        eps_c = sb("eps_c", [P, 1], F32)
        ones_r = sb("ones_r", [1, P], BF16)
        arena16 = sb("arena16", [P, 16384], BF16)
        xb16 = arena16[:, 0:16384]
        h1 = arena16[:, 0:8192]
        r2c16f = arena16[:, 8192:10240]
        r2sq16f = arena16[:, 10240:12288]
        r16d = [arena16[:, 12288 + i * 256:12288 + (i + 1) * 256]
                for i in range(2)] + \
               [arena16[:, 11264 + i * 256:11264 + (i + 1) * 256]
                for i in range(2)]
        rsq16d = [arena16[:, 12800 + i * 256:12800 + (i + 1) * 256]
                  for i in range(2)] + \
                 [arena16[:, 11776 + i * 256:11776 + (i + 1) * 256]
                  for i in range(2)]
        rr = [arena16[:, 13312 + i * 256:13312 + (i + 1) * 256]
              for i in range(4)] + [
              arena16[:, 15360 + i * 256:15360 + (i + 1) * 256]
              for i in range(2)]
        mi16 = [arena16[0:1, 14336 + i * 512:14336 + (i + 1) * 512]
                for i in range(2)]
        arena32 = sb("arena32", [P, 2816], F32)
        xp = arena32[:, 0:2048]
        vsb = [arena32[:, 0 + i * 256:0 + (i + 1) * 256] for i in range(2)]
        sacc = [arena32[:, 512 + i * 256:512 + (i + 1) * 256] for i in range(2)]
        xs_d = [arena32[:, 1024 + i * 256:1024 + (i + 1) * 256] for i in range(2)]
        stA = [arena32[0:1, 1536 + i * 256:1536 + (i + 1) * 256] for i in range(2)]
        stB = [arena32[0:1, 2048 + i * 256:2048 + (i + 1) * 256] for i in range(2)]
        tmp_row = arena32[0:1, 2560:2816]
        out_st = xs_d
        hs16 = sb("hs16", [P, L], BF16)
        r_sb = sb("r_sb", [P, NDC * TTOK], F32)
        r2_sb = sb("r2_sb", [P, NDC * TTOK], F32)
        x1 = [sb("x1_0", [P, NDC * TTOK], F32),
              sb("x1_1", [P, NDC * TTOK], F32)]
        x1b = sb("x1b", [P, NDC * TTOK], BF16)

        pbf = [psum("pbf0"), psum("pbf1")]
        pby = [psum("pby0"), psum("pby1")]
        pb_s1 = psum("pb_s1")
        pb_s2 = psum("pb_s2")
        pb_b1 = psum("pb_b1")
        pb_b2 = psum("pb_b2")

        s_pe = sem("s_pe")
        s_gp = sem("s_gp")
        s_ve = sem("s_ve")
        s_ac = sem("s_ac")
        s_dw1 = sem("s_dw1")
        s_dwf = sem("s_dwf")
        s_dwp = sem("s_dwp")
        s_c = sem("s_c")
        s_x0 = sem("s_x0")
        s_x1 = sem("s_x1")
        s_o = [sem("s_o0"), sem("s_o1")]

        SEMS = {"pe": s_pe, "gp": s_gp, "ve": s_ve, "ac": s_ac, "dw1": s_dw1,
                "dwf": s_dwf, "dwp": s_dwp, "c": s_c, "x0": s_x0,
                "x1": s_x1, "o0": s_o[0], "o1": s_o[1]}

        sched = []

        ent_ctx = {"label": ""}

        def ent(engine, build, ev=None, waits=(), dma_sem=None):
            e = _sched_entry(engine, build, ev, waits, dma_sem)
            e["label"] = ev or ent_ctx["label"]
            sched.append(e)

        ent("gp", lambda g: g.dma_start(
            out=bw16[:].rearrange("p (c n) -> p c n", c=NDC),
            in_=bwT.rearrange("(c p) n -> p c n", p=P)),
            ev="dw_bw", dma_sem="dw1")
        ent("gp", lambda g: g.dma_start(out=cw16[:], in_=cwT),
            ev="dw_cw", dma_sem="dw1")
        ent("gp", lambda g: g.dma_start(
            out=xb16.rearrange("p (c t) -> p c t", c=NDC),
            in_=xT.rearrange("(c p) t -> p c t", p=P)),
            ev="dw_xb", dma_sem="dw1")
        for q in range(NDC):
            src = wfcT.rearrange("(c p) f -> p c f", p=P)[:, q:q + 1, :]
            dst = wfc16[:, q * FF:(q + 1) * FF]
            ent("gp", (lambda g, s=src, d=dst: g.dma_start(
                out=d.rearrange("p (c f) -> p c f", c=1), in_=s)),
                ev=(f"dw_wfc" if q == NDC - 1 else f"dw_wfc{q}"), dma_sem="dwf")
        for q in range(8):
            qs = NFC // 8
            src = wprojT.rearrange("(c p) d -> p c d", p=P)[:, q * qs:(q + 1) * qs, :]
            dst = wpj16[:, q * qs * DM:(q + 1) * qs * DM]
            ent("gp", (lambda g, s=src, d=dst: g.dma_start(out=d, in_=s)),
                ev=f"dw_wp{q}", dma_sem="dwp")
        ent("sy", lambda s: s.dma_start(out=logA_sb[:], in_=logA),
            ev="c_logA", dma_sem="c")
        ent("sy", lambda s: s.dma_start(out=dre_sb[:], in_=dre),
            ev="c_dre", dma_sem="c")
        ent("sy", lambda s: s.dma_start(
            out=r_sb[:].rearrange("p (c t) -> p c t", t=TTOK),
            in_=xT.rearrange("(c p) t -> p c t", p=P)[:, :, 0:TTOK]),
            ev="c_xr0", dma_sem="c")


        ent("ve", lambda v: v.memset(ones_c[:], 1.0), ev="ones_c")
        ent("ve", lambda v: v.memset(eps_c[:], EPS), ev="eps_c")
        ent("ve", lambda v: v.memset(ones_r[:], 1.0), ev="ones_r")
        ent("ac", lambda a: a.activation(out=acol[:], in_=logA_sb[:], func=AF.Exp),
            ev="A", waits=["c_xr0"])
        ent("ac", lambda a: a.activation(out=d1p[:], in_=dre_sb[:],
                                         func=AF.Identity, bias=1.0),
            ev="D1", waits=["c_xr0"])

        for tq in range(4):
            waits = ["dw_xb"] if tq == 0 else [f"xpev{tq - 2}"] if tq >= 2 else []
            for dc in range(NDC):
                w = list(waits) if dc == 0 else []
                ent("pe", (lambda t, tq=tq, dc=dc: t.matmul(
                    pbf[tq % 2][:, 0:512],
                    bw16[:, dc * P:(dc + 1) * P],
                    xb16[:, dc * L + tq * 512: dc * L + (tq + 1) * 512],
                    start=(dc == 0), stop=(dc == NDC - 1))),
                    ev=(f"xpmm{tq}" if dc == NDC - 1 else None), waits=w)
            ent("ac", (lambda a, tq=tq: a.activation(
                out=xp[:, tq * 512:(tq + 1) * 512], in_=pbf[tq % 2][:, 0:512],
                func=AF.Copy)), ev=f"xpev{tq}", waits=[f"xpmm{tq}"])


        ent("ve", lambda v: v.tensor_tensor_scan(
            out=hs16[:], data0=acol[:].broadcast_to([P, L]), data1=xp[:],
            initial=0.0, op0=ALU.mult, op1=ALU.add),
            ev="scan", waits=["xpev3", "A"])


        def pe_y(tau, dcs=range(NDC)):
            for dc in dcs:
                slot = dc if dc < 6 else dc - 6
                w = []
                if tau == 0 and dc == 0:
                    w.append("scan")
                if dc >= 6:
                    w.append(f"epi({tau},{dc - 6})")
                elif tau > 0:
                    w.append(f"epi({tau - 1},{dc + 6})" if dc < 2 else None)
                if slot < 4:
                    half = (slot // 2) % 2
                    if tau > 0 and half == 0:
                        w.append(f"mq1({tau - 1})")
                    if tau > 1 and half == 1:
                        w.append(f"mq2({tau - 2})")
                    dst = YB[slot % 2][:, half * 256:half * 256 + 256]
                else:
                    if tau > 0:
                        w.append(f"ln1({tau - 1},{NDC - 1})")
                    dst = pb_b1[:, (slot - 4) * 256:(slot - 4) * 256 + 256]
                w = [x for x in w if x]
                ent("pe", (lambda t, tau=tau, dc=dc, dst=dst: t.matmul(
                    dst,
                    cw16[:, dc * P:(dc + 1) * P],
                    hs16[:, tau * TTOK:(tau + 1) * TTOK],
                    start=True, stop=True)),
                    ev=f"y({tau},{dc})", waits=w)

        def dve_epi(tau, dcs=range(NDC)):
            for dc in dcs:
                slot = dc if dc < 6 else dc - 6
                if slot < 4:
                    half = (slot // 2) % 2
                    src = YB[slot % 2][:, half * 256:half * 256 + 256]
                else:
                    src = pb_b1[:, (slot - 4) * 256:(slot - 4) * 256 + 256]
                sl = slice(dc * TTOK, (dc + 1) * TTOK)
                if tau == 0:
                    w = [f"y({tau},{dc})", "D1", "c_xr0"]
                    ent("ve", (lambda v, dc=dc, sl=sl: v.tensor_scalar(
                        out=r_sb[:, sl], in0=r_sb[:, sl],
                        scalar1=d1p[:, dc:dc + 1], scalar2=None, op0=ALU.mult)),
                        waits=w)
                else:
                    w = [f"y({tau},{dc})", "D1", f"xsd({tau},{dc})"]
                    ent("ve", (lambda v, dc=dc, sl=sl: v.tensor_scalar(
                        out=r_sb[:, sl], in0=xs_d[dc % 2][:],
                        scalar1=d1p[:, dc:dc + 1], scalar2=None, op0=ALU.mult)),
                        waits=w)
                ent("ve", (lambda v, sl=sl, src=src: v.tensor_tensor(
                    out=r_sb[:, sl], in0=r_sb[:, sl],
                    in1=src, op=ALU.add)),
                    ev=f"epi({tau},{dc})")

        def act_rcasts(tau, dcs=range(NDC)):
            for dc in dcs:
                sl = slice(dc * TTOK, (dc + 1) * TTOK)
                w = [f"epi({tau},{dc})"]
                if tau > 0 or dc >= 4:
                    w.append(f"st1mm({tau},{dc - 4})" if dc >= 4
                             else f"st1mm({tau - 1},{dc + 4})")
                ent("ac", (lambda a, dc=dc, sl=sl: a.activation(
                    out=r16d[dc % 4][:], in_=r_sb[:, sl], func=AF.Copy)),
                    waits=w)
                ent("ac", (lambda a, dc=dc, sl=sl: a.activation(
                    out=rsq16d[dc % 4][:], in_=r_sb[:, sl], func=AF.Square)),
                    ev=f"r16({tau},{dc})")

        def pe_stats1(tau, dcs=range(NDC)):
            for dc in dcs:
                w = [f"r16({tau},{dc})"]
                if dc == 0:
                    w.append(f"epi({tau},{NDC - 1})")
                    w.append(f"mq1({tau - 1})" if tau > 0 else None)
                w = [x for x in w if x]
                ent("pe", (lambda t, dc=dc: t.matmul(
                    pb_s1[0:1, 0:256], ones_c[:], r16d[dc % 4][:],
                    start=(dc == 0), stop=(dc == NDC - 1))), waits=w)
            for dc in dcs:
                ent("pe", (lambda t, dc=dc: t.matmul(
                    pb_s2[0:1, 0:256], ones_c[:], rsq16d[dc % 4][:],
                    start=(dc == 0), stop=(dc == NDC - 1))),
                    ev=f"st1mm({tau},{dc})", waits=[f"r16({tau},{dc})"])

        def math_ln_a(tau, which):
            off = 0 if which == 1 else 256
            i = which - 1
            suf = f"{which}({tau})"
            ent("ac", (lambda a: a.activation(
                out=stA[i][:], in_=pb_s1[0:1, off:off + 256], func=AF.Copy,
                scale=1.0 / DM)), waits=[f"st{which}mm({tau},{NDC - 1})"])
            ent("ac", (lambda a: a.activation(
                out=mi16[i][0:1, 0:TTOK], in_=stA[i][:], func=AF.Copy)))
            ent("ac", (lambda a: a.activation(
                out=stB[i][:], in_=pb_s2[0:1, off:off + 256], func=AF.Copy,
                scale=1.0 / DM)), ev=f"mq{suf}")

        def math_ln_b(tau, which):
            i = which - 1
            suf = f"{which}({tau})"
            ent("ve", (lambda v, i=i: v.tensor_tensor(
                out=tmp_row[:], in0=stA[i][:], in1=stA[i][:], op=ALU.mult)),
                waits=[f"mq{suf}"])
            ent("ve", (lambda v, i=i: v.tensor_tensor(
                out=stB[i][:], in0=stB[i][:], in1=tmp_row[:], op=ALU.subtract)),
                ev=f"var{suf}")

        def math_ln_c(tau, which):
            i = which - 1
            suf = f"{which}({tau})"
            ent("ac", (lambda a, i=i: a.activation(
                out=tmp_row[:], in_=stB[i][:], func=AF.Ln, bias=eps_c[0:1, :])),
                waits=[f"var{suf}"])
            ent("ac", (lambda a, i=i: a.activation(
                out=stA[i][:], in_=tmp_row[:], func=AF.Exp, scale=-0.5)))
            ent("ac", (lambda a, i=i: a.activation(
                out=mi16[i][0:1, TTOK:2 * TTOK], in_=stA[i][:], func=AF.Copy)),
                ev=f"lm{suf}")

        def math_ln(tau, which):
            math_ln_a(tau, which)
            math_ln_b(tau, which)
            math_ln_c(tau, which)

        def pe_bcast(tau, which):
            pb = pb_b1 if which == 1 else pb_b2
            i = which - 1
            w = [f"lm{which}({tau})"]
            if which == 1 and tau > 0:
                w.append(f"ln1({tau - 1},{NDC - 1})")
            if which == 1:
                w.append(f"epi({tau},5)")
            if which == 2 and tau > 0:
                w.append(f"ln2({tau - 1},{NDC - 1})")
            ent("pe", (lambda t: t.matmul(
                pb[:, 0:256], ones_r[:], mi16[i][0:1, 0:TTOK],
                start=True, stop=True)), waits=w)
            ent("pe", (lambda t: t.matmul(
                pb[:, 256:512], ones_r[:], mi16[i][0:1, TTOK:2 * TTOK],
                start=True, stop=True)), ev=f"bc{which}({tau})")

        def dve_ln1_apply(tau, dcs=range(NDC)):
            xw = x1[tau % 2]
            for dc in dcs:
                sl = slice(dc * TTOK, (dc + 1) * TTOK)
                w = [f"bc1({tau})"] if dc == dcs[0] else []
                ent("ve", (lambda v, sl=sl, xw=xw: v.tensor_tensor(
                    out=xw[:, sl], in0=r_sb[:, sl], in1=pb_b1[:, 0:256],
                    op=ALU.subtract)), waits=w)
                ent("ve", (lambda v, sl=sl, xw=xw: v.tensor_tensor(
                    out=xw[:, sl], in0=xw[:, sl], in1=pb_b1[:, 256:512],
                    op=ALU.mult)), ev=f"ln1({tau},{dc})")

        def act_x1b(tau, dcs=range(NDC)):
            xw = x1[tau % 2]
            for dc in dcs:
                sl = slice(dc * TTOK, (dc + 1) * TTOK)
                ent("ac", (lambda a, sl=sl, xw=xw: a.activation(
                    out=x1b[:, sl], in_=xw[:, sl], func=AF.Copy)),
                    ev=f"x1b({tau},{dc})", waits=[f"ln1({tau},{dc})"])

        def pe_ffn1_fc(tau, fc):
            w = []
            if fc == 0:
                w += [f"x1b({tau},{NDC - 1})", "dw_wfc"]
            if fc < 4 and tau > 0:
                w.append(f"vev({tau - 1},{fc + 4})")
            elif fc >= 4:
                w.append(f"relu({tau},{fc - 4})")
            for dc in range(NDC):
                ww = w if dc == 0 else []
                ent("pe", (lambda t, fc=fc, dc=dc: t.matmul(
                    FB[fc % 4][:, 0:256],
                    wfc16[:, dc * FF + fc * P: dc * FF + (fc + 1) * P],
                    x1b[:, dc * TTOK:(dc + 1) * TTOK],
                    start=(dc == 0), stop=(dc == NDC - 1))),
                    ev=(f"f1({tau},{fc})" if dc == NDC - 1 else None),
                    waits=ww)

        def act_relu(tau, fc):
            ent("ac", (lambda a, fc=fc: a.activation(
                out=rr[fc % 6][:], in_=FB[fc % 4][:, 0:256], func=AF.Relu)),
                ev=f"relu({tau},{fc})",
                waits=[f"f1({tau},{fc})"] + ([f"h1m({tau},{fc - 6})"] if fc >= 6 else []))

        def dve_h1m(tau, fc):
            ent("ve", (lambda v, fc=fc: v.tensor_tensor(
                out=h1[:, fc * TTOK:(fc + 1) * TTOK], in0=rr[fc % 6][:],
                in1=rr[fc % 6][:], op=ALU.mult)),
                ev=f"h1m({tau},{fc})", waits=[f"relu({tau},{fc})"])

        def pe_ffn2_dc(tau, dc):
            w = []
            if dc == 0:
                w += [f"h1m({tau},{NFC - 4})", "dw_wp7"]
            if dc < 4:
                w.append(f"relu({tau},{NFC - 4 + dc})")
            else:
                w.append(f"vev({tau},{dc - 4})")
            for fc in range(NFC):
                ww = w if fc == 0 else (
                    [f"h1m({tau},{NFC - 1})"] if (dc == 0 and fc == NFC - 3)
                    else [])
                ent("pe", (lambda t, dc=dc, fc=fc: t.matmul(
                    FB[dc % 4][:, 0:256],
                    wpj16[:, fc * DM + dc * P: fc * DM + (dc + 1) * P],
                    h1[:, fc * TTOK:(fc + 1) * TTOK],
                    start=(fc == 0), stop=(fc == NFC - 1))),
                    ev=(f"f2({tau},{dc})" if fc == NFC - 1 else None),
                    waits=ww)

        def act_vev(tau, dc):
            w = [f"f2({tau},{dc})"]
            if dc >= 2:
                w.append(f"spk({tau},{dc - 2})")
            ent("ac", (lambda a, dc=dc: a.activation(
                out=vsb[dc % 2][:], in_=FB[dc % 4][:, 0:256], func=AF.Copy)),
                ev=f"vev({tau},{dc})", waits=w)

        def dve_spike(tau, dc):
            sl = slice(dc * TTOK, (dc + 1) * TTOK)
            b = dc % 2
            w = [f"vev({tau},{dc})"]
            ent("ve", (lambda v, b=b: v.tensor_scalar(
                out=sacc[b][:], in0=vsb[b][:], scalar1=B_SPIKE[0], scalar2=None,
                op0=ALU.is_gt)), waits=w)
            for k in range(1, 5):
                ent("ve", (lambda v, b=b, k=k: v.scalar_tensor_tensor(
                    out=sacc[b][:], in0=vsb[b][:], scalar=B_SPIKE[k],
                    in1=sacc[b][:], op0=ALU.is_gt, op1=ALU.add)))
            xw = x1[tau % 2]
            ent("ve", (lambda v, b=b, sl=sl, xw=xw: v.scalar_tensor_tensor(
                out=r2_sb[:, sl], in0=sacc[b][:], scalar=0.2, in1=xw[:, sl],
                op0=ALU.mult, op1=ALU.add)), ev=f"spk({tau},{dc})")

        def act_r2casts(tau, dcs):
            for dc in dcs:
                sl = slice(dc * TTOK, (dc + 1) * TTOK)
                w = [f"spk({tau},{dc})"]
                if dc == 0 and tau > 0:
                    w.append(f"st2mm({tau - 1},{NDC - 1})")
                ent("ac", (lambda a, sl=sl: a.activation(
                    out=r2c16f[:, sl], in_=r2_sb[:, sl], func=AF.Copy)),
                    ev=f"r2cc({tau},{dc})", waits=w)
                ent("ac", (lambda a, sl=sl: a.activation(
                    out=r2sq16f[:, sl], in_=r2_sb[:, sl], func=AF.Square)),
                    ev=f"r2c({tau},{dc})")

        def pe_stats2(tau):
            for dc in range(NDC):
                w = [f"r2cc({tau},{dc})"]
                if dc == 0 and tau > 0:
                    w.append(f"mq2({tau - 1})")
                ent("pe", (lambda t, dc=dc: t.matmul(
                    pb_s1[0:1, 256:512], ones_c[:],
                    r2c16f[:, dc * TTOK:(dc + 1) * TTOK],
                    start=(dc == 0), stop=(dc == NDC - 1))), waits=w)
            for dc in range(NDC):
                ent("pe", (lambda t, dc=dc: t.matmul(
                    pb_s2[0:1, 256:512], ones_c[:],
                    r2sq16f[:, dc * TTOK:(dc + 1) * TTOK],
                    start=(dc == 0), stop=(dc == NDC - 1))),
                    ev=f"st2mm({tau},{dc})", waits=[f"r2c({tau},{dc})"])

        def dve_ln2_apply(tau):
            for dc in range(NDC):
                sl = slice(dc * TTOK, (dc + 1) * TTOK)
                b = dc % 2
                w = [f"bc2({tau})"] if dc == 0 else []
                nprev = tau * NDC + dc - 2
                if nprev >= 0:
                    w.append(f"outd_tot({nprev})")
                ent("ve", (lambda v, sl=sl, b=b: v.tensor_tensor(
                    out=out_st[b][:], in0=r2_sb[:, sl], in1=pb_b2[:, 0:256],
                    op=ALU.subtract)), waits=w)
                ent("ve", (lambda v, b=b: v.tensor_tensor(
                    out=out_st[b][:], in0=out_st[b][:], in1=pb_b2[:, 256:512],
                    op=ALU.mult)), ev=f"ln2({tau},{dc})")
                ent("sy", (lambda s, tau=tau, dc=dc, b=b: s.dma_start(
                    out=out_d[dc * P:(dc + 1) * P, tau * TTOK:(tau + 1) * TTOK],
                    in_=out_st[b][:])),
                    ev=f"outd({tau},{dc})", dma_sem=f"o{b}",
                    waits=[f"ln2({tau},{dc})"])

        def sync_xs(tau, dcs, first=False):
            for dc in dcs:
                if dc >= 2:
                    w = [f"epi({tau},{dc - 2})"]
                elif tau == 0:
                    w = ["scan"]
                else:
                    w = [f"epi({tau - 1},{dc + 6})"]
                    if tau >= 3:
                        w.append(f"outd_tot({(tau - 3) * NDC + 6 + dc})")
                src = xT.rearrange("(c p) t -> p c t", p=P)[
                    :, dc, tau * TTOK:(tau + 1) * TTOK]
                ent("sy", (lambda s, src=src, dc=dc: s.dma_start(
                    out=xs_d[dc % 2][:], in_=src)),
                    ev=f"xsd({tau},{dc})", dma_sem=f"x{dc % 2}", waits=w)

        FB = [pbf[0], pbf[1], pby[0], pby[1]]
        YB = [pb_s1, pb_s2]
        nc._sched = sched
        pe_y(0, range(0, 4))
        dve_epi(0, range(0, 4))
        pe_y(0, range(4, NDC))
        dve_epi(0, range(4, NDC))
        act_rcasts(0)
        pe_stats1(0, range(0, 4))
        pe_stats1(0, range(4, NDC))
        math_ln(0, 1)
        pe_bcast(0, 1)
        dve_ln1_apply(0)
        act_x1b(0)
        if not SAFE:
            for q in range(2, NDC):
                ent("pe", (lambda t, q=q: t.matmul(
                    pb_b2[:, 0:256], cw16[:, 0:P], cw16[:, 256:512],
                    start=True, stop=True)),
                    waits=([f"dw_wfc{q}"] if q < NDC - 1 else ["dw_wfc"]))

        for tau in range(NT):
            nx = tau + 1
            for fc in range(NFC):
                pe_ffn1_fc(tau, fc)
                act_relu(tau, fc)
                dve_h1m(tau, fc)
                if tau > 0 and fc == 2:
                    pe_stats2(tau - 1)
                if tau > 0 and fc == 3:
                    math_ln_a(tau - 1, 2)
                    math_ln_b(tau - 1, 2)
                if tau > 0 and fc == 5:
                    math_ln_c(tau - 1, 2)
                if nx < NT and fc == 2:
                    sync_xs(nx, range(0, 2))
                if nx < NT and fc == 4:
                    pe_y(nx, range(0, 2))
                if nx < NT and fc == 6:
                    pe_y(nx, range(2, 4))
                if nx < NT and fc == 7:
                    dve_epi(nx, range(0, 2))
                    sync_xs(nx, range(2, 4))
                if nx < NT and fc == 9:
                    pe_y(nx, range(4, NDC))
                if tau > 0 and fc == 10:
                    pe_bcast(tau - 1, 2)
                if nx < NT and fc == 11:
                    dve_epi(nx, range(2, 4))
                    sync_xs(nx, range(4, 6))
                if nx < NT and fc == 15:
                    dve_epi(nx, range(4, 6))
                    sync_xs(nx, range(6, NDC))
                if nx < NT and fc == 19:
                    dve_epi(nx, range(6, NDC))


            if tau > 0:
                dve_ln2_apply(tau - 1)
            pe_ffn2_dc(tau, 0)
            if nx < NT:
                act_rcasts(nx)
            act_vev(tau, 0)
            if nx < NT:
                pe_stats1(nx, range(0, 4))
            pe_ffn2_dc(tau, 1)
            act_vev(tau, 1)
            if nx < NT:
                pe_stats1(nx, range(4, NDC))
                math_ln_a(nx, 1)
            pe_ffn2_dc(tau, 2)
            dve_spike(tau, 0)
            if nx < NT:
                math_ln_b(nx, 1)
                math_ln_c(nx, 1)
            act_vev(tau, 2)
            pe_ffn2_dc(tau, 3)
            dve_spike(tau, 1)
            act_vev(tau, 3)
            if nx < NT:
                pe_bcast(nx, 1)
            pe_ffn2_dc(tau, 4)
            dve_spike(tau, 2)
            act_vev(tau, 4)
            if nx < NT:
                dve_ln1_apply(nx, range(0, 4))
                act_x1b(nx, range(0, 4))
            pe_ffn2_dc(tau, 5)
            dve_spike(tau, 3)
            act_vev(tau, 5)
            if nx < NT:
                dve_ln1_apply(nx, range(4, NDC))
                act_x1b(nx, range(4, NDC))
            pe_ffn2_dc(tau, 6)
            dve_spike(tau, 4)
            act_vev(tau, 6)
            act_r2casts(tau, range(0, 2))
            pe_ffn2_dc(tau, 7)
            dve_spike(tau, 5)
            act_vev(tau, 7)
            act_r2casts(tau, range(2, 4))
            dve_spike(tau, 6)
            act_r2casts(tau, range(4, 6))
            dve_spike(tau, 7)
            act_r2casts(tau, range(6, NDC))

        pe_stats2(NT - 1)
        math_ln(NT - 1, 2)
        pe_bcast(NT - 1, 2)
        dve_ln2_apply(NT - 1)

        counters = {k: 0 for k in SEMS}
        events = {}
        own = {"ve": 0, "ac": 0}
        for e in sched:
            eng = e["engine"]
            if e["dma_sem"] is not None:
                counters[e["dma_sem"]] += 16
                if e["ev"]:
                    events[e["ev"]] = (e["dma_sem"], counters[e["dma_sem"]])
            elif eng == "pe":
                if e["ev"]:
                    counters["pe"] += 1
                    assert e["ev"] not in events, f"dup event {e['ev']}"
                    events[e["ev"]] = ("pe", counters["pe"])
            elif eng in own or eng == "gp":
                semkey = "gp" if eng == "gp" else eng
                counters[semkey] += 1
                if e["ev"]:
                    events[e["ev"]] = (semkey, counters[semkey])
            else:
                assert e["ev"] is None, e["ev"]
        par_count = {0: 0, 1: 0}
        for tau in range(NT):
            for dc in range(NDC):
                b = dc % 2
                par_count[b] += 16
                events[f"outd_tot({tau * NDC + dc})"] = (f"o{b}", par_count[b])

        needed = {}
        for e in sched:
            for wname in e["waits"]:
                semkey, val = events[wname]
                needed.setdefault(semkey, set()).add(val)

        block = ctx.enter_context(nc.Block())

        def emit_for(engine_key, eng):
            own_count = 0
            pending = [0]
            pe_count = [0]
            wm = {}
            for e in sched:
                if e["engine"] != engine_key:
                    continue
                for wname in e["waits"]:
                    semkey, val = events[wname]
                    if wm.get(semkey, -1) >= val:
                        continue
                    wm[semkey] = val
                    eng.wait_ge(SEMS[semkey], val)
                if SAFE and engine_key in ("ve", "ac", "gp") and own_count > 0:
                    eng.wait_ge(SEMS[engine_key], own_count)
                inst = e["build"](eng)
                if e["dma_sem"] is not None:
                    inst.then_inc(SEMS[e["dma_sem"]], 16)
                elif engine_key in ("ve", "ac", "gp"):
                    own_count += 1
                    inst.then_inc(SEMS[engine_key], 1)
                elif engine_key == "pe" and e["ev"]:
                    inst.then_inc(SEMS["pe"], 1)

        @block.tensor
        def _(t):
            emit_for("pe", t)

        @block.vector
        def _(v):
            emit_for("ve", v)

        @block.scalar
        def _(a):
            emit_for("ac", a)

        @block.gpsimd
        def _(g):
            emit_for("gp", g)

        @block.sync
        def _(s):
            emit_for("sy", s)
            s.wait_ge(SEMS["o0"], par_count[0])
            s.wait_ge(SEMS["o1"], par_count[1])

    return nc


_NC_CACHE = None


def kernel(L0_attn_x, L0_attn_B_w, L0_attn_C_w, L0_attn_log_A, L0_attn_D_param,
           L0_ffn_W_fc, L0_ffn_W_proj):
    global _NC_CACHE
    if _NC_CACHE is None:
        _NC_CACHE = build_nc()
    nc = _NC_CACHE

    x = np.asarray(L0_attn_x)
    bwT = np.ascontiguousarray(np.asarray(L0_attn_B_w).T)
    cwT = np.ascontiguousarray(np.asarray(L0_attn_C_w).T)
    logA = np.ascontiguousarray(np.asarray(L0_attn_log_A).reshape(P, 1))
    dre = np.ascontiguousarray(np.asarray(L0_attn_D_param).reshape(NDC, P).T)
    wfcT = np.ascontiguousarray(np.asarray(L0_ffn_W_fc).T)
    wprojT = np.ascontiguousarray(np.asarray(L0_ffn_W_proj).T)

    in_maps = []
    for c in range(8):
        in_maps.append({
            "xT": np.ascontiguousarray(x[c].T),
            "bwT": bwT, "cwT": cwT, "logA": logA, "dre": dre,
            "wfcT": wfcT, "wprojT": wprojT,
        })
    res = run_bass_kernel_spmd(nc, in_maps, core_ids=list(range(8)))
    out = np.stack([np.asarray(r["out"]).T for r in res.results], axis=0)
    return np.ascontiguousarray(out.astype(np.float32))
